# version 08
# baseline (speedup 1.0000x reference)
import math
from contextlib import ExitStack

import numpy as np

import concourse.bass as bass
import concourse.mybir as mybir
from concourse.bass_utils import run_bass_kernel_spmd

F32 = mybir.dt.float32
BF16 = mybir.dt.bfloat16
AF = mybir.ActivationFunctionType
ALU = mybir.AluOpType
AX = mybir.AxisListType

P = 128
D = 2048
TOK = 1024
S = 4096
NT = TOK // P
DT = D // P
H = 16
NKT = S // P
E = 8
INV_C = 1.0 / math.sqrt(128.0)
EPS = 1e-5

_DBG = set()
_STOP = "Z"


def _ap(t, offset, dims):
    return bass.AP(t, offset, dims)


class _SER:

    _WR = {"out", "accum_out"}
    _PASS = {"wait_ge", "drain", "nop", "sem_inc", "dma_start",
             "dma_start_transpose", "collective_compute", "wait_op",
             "memzero"}

    def __init__(self, e):
        self._e = e
        self._dw = set()
        self._dr = set()

    def __getattr__(self, name):
        f = getattr(self._e, name)
        if name in self._PASS or not callable(f):
            return f

        def wrapped(*args, **kw):
            ws, rs = [], []
            for i, a in enumerate(args):
                if isinstance(a, bass.AP):
                    (ws if i == 0 else rs).append(a.tensor.name)
            for k, a in kw.items():
                if isinstance(a, bass.AP):
                    (ws if k in self._WR else rs).append(a.tensor.name)
            if (self._dw & set(ws + rs)) or (self._dr & set(ws)):
                self._e.drain()
                self._dw.clear()
                self._dr.clear()
            self._dw.update(ws)
            self._dr.update(rs)
            return f(*args, **kw)

        return wrapped


def build_nc():
    nc = bass.Bass("TRN2", target_bir_lowering=False)

    x_own = nc.dram_tensor("x_own", [TOK, D], F32, kind="ExternalInput")
    wqT = nc.dram_tensor("wqT", [D, D], F32, kind="ExternalInput")
    wkT = nc.dram_tensor("wkT", [D, D], F32, kind="ExternalInput")
    wvT = nc.dram_tensor("wvT", [D, D], F32, kind="ExternalInput")
    woT = nc.dram_tensor("woT", [D, D], F32, kind="ExternalInput")
    b_q = nc.dram_tensor("b_q", [P, DT], F32, kind="ExternalInput")
    b_k = nc.dram_tensor("b_k", [P, DT], F32, kind="ExternalInput")
    b_v = nc.dram_tensor("b_v", [P, DT], F32, kind="ExternalInput")
    b_o = nc.dram_tensor("b_o", [P, DT], F32, kind="ExternalInput")
    wr = nc.dram_tensor("wr", [D, E], F32, kind="ExternalInput")
    wg = nc.dram_tensor("wg", [E, D, D], F32, kind="ExternalInput")
    wu = nc.dram_tensor("wu", [E, D, D], F32, kind="ExternalInput")
    wdn = nc.dram_tensor("wdn", [E, D, D], F32, kind="ExternalInput")
    out = nc.dram_tensor("out", [TOK, D], F32, kind="ExternalOutput")

    q_loc = nc.dram_tensor("q_loc", [D, TOK], BF16, kind="Internal")
    k_loc = nc.dram_tensor("k_loc", [TOK, D], BF16, kind="Internal")
    v_loc = nc.dram_tensor("v_loc", [TOK, D], BF16, kind="Internal")
    k_full = nc.dram_tensor("k_full", [S, D], BF16, kind="Internal")
    v_full = nc.dram_tensor("v_full", [S, D], BF16, kind="Internal")

    dbg = {}
    for name, shape, dt in (
        ("dbg_q", [P, H, TOK], BF16),
        ("dbg_kv", [TOK, 2 * D], BF16),
        ("dbg_lnt", [P, DT, TOK], BF16),
        ("dbg_ctx", [P, H, TOK], BF16),
        ("dbg_x1", [P, NT, D], F32),
        ("dbg_w8", [P, E, NT], F32),
        ("dbg_ln2", [P, DT, TOK], BF16),
    ):
        if name in _DBG:
            dbg[name] = nc.dram_tensor(name, shape, dt, kind="ExternalOutput")

    es = ExitStack()
    X_res = es.enter_context(nc.sbuf_tensor("X_res", [P, NT, D], F32))
    actT = es.enter_context(nc.sbuf_tensor("actT", [P, DT, TOK], BF16))
    scr = es.enter_context(nc.sbuf_tensor("scr", [P, 16384], BF16))
    bq_sb = es.enter_context(nc.sbuf_tensor("bq_sb", [P, DT], F32))
    bk_sb = es.enter_context(nc.sbuf_tensor("bk_sb", [P, DT], F32))
    bv_sb = es.enter_context(nc.sbuf_tensor("bv_sb", [P, DT], F32))
    bo_sb = es.enter_context(nc.sbuf_tensor("bo_sb", [P, DT], F32))
    ident = es.enter_context(nc.sbuf_tensor("ident", [P, P], BF16))
    c_sum = es.enter_context(nc.sbuf_tensor("c_sum", [P, NT], F32))
    c_ssq = es.enter_context(nc.sbuf_tensor("c_ssq", [P, NT], F32))
    c_negm = es.enter_context(nc.sbuf_tensor("c_negm", [P, NT], F32))
    c_rstd = es.enter_context(nc.sbuf_tensor("c_rstd", [P, NT], F32))
    c_t0 = es.enter_context(nc.sbuf_tensor("c_t0", [P, NT], F32))
    c_t1 = es.enter_context(nc.sbuf_tensor("c_t1", [P, NT], F32))
    wr_sb = es.enter_context(nc.sbuf_tensor("wr_sb", [P, DT, E], F32))
    wr_hi = es.enter_context(nc.sbuf_tensor("wr_hi", [P, DT, E], BF16))
    wr_lo = es.enter_context(nc.sbuf_tensor("wr_lo", [P, DT, E], BF16))
    W_sel = es.enter_context(nc.sbuf_tensor("W_sel", [P, E, NT], F32))

    junk = scr[:, 0:D]
    xst = [scr[:, D + i * D: D + (i + 1) * D] for i in range(2)]
    kvst = [scr[:, 3 * D + i * 512: 3 * D + (i + 1) * 512] for i in range(4)]
    PT4 = scr[:, :].rearrange("p (a b) -> p a b", a=NKT)
    hlT = scr[:, :].rearrange("p (a b) -> p a b", a=DT)

    def sem(name):
        return es.enter_context(nc.semaphore(name))

    def emit_G():
        gO = sem("gO")
        with nc.Block() as blk:
            @blk.sync
            def _(sy):
                sy.dma_start(out=_ap(out, 0, [[D, P], [P * D, NT], [1, D]]),
                             in_=X_res[:, :, :]).then_inc(gO, 16)
                sy.wait_ge(gO, 16)

    def strip_src(mat, oc, expert=None):
        base = oc * 512
        if expert is not None:
            base += expert * D * D
        return _ap(mat, base, [[D, P], [P * D, DT], [1, 512]])

    aId = sem("aId")
    aX = sem("aX")
    aB = sem("aB")
    aWs = [sem("aW0"), sem("aW1")]
    aS1 = sem("aS1")
    aV2 = sem("aV2")
    aS2 = sem("aS2")
    aV3 = sem("aV3")
    aTt = sem("aTt")
    aEv = sem("aEv")
    aPm = sem("aPm")
    aPe = sem("aPe")
    aKVs = [sem(f"aKV{i}") for i in range(4)]
    ccK = sem("ccK")
    ccV = sem("ccV")


    a_strips = ([strip_src(wkT, oc) for oc in range(4)]
                + [strip_src(wvT, oc) for oc in range(4)]
                + [strip_src(wqT, oc) for oc in range(4)])
    NG = 96

    def a_group(idx):
        if idx < 32:
            oc, tt = divmod(idx, 8)
            return ("k", oc, oc, tt, None)
        if idx < 64:
            oc, tt = divmod(idx - 32, 8)
            return ("v", 4 + oc, oc, tt, None)
        qi = idx - 64
        oc, r = divmod(qi, 8)
        os_, g2 = divmod(r, 2)
        return ("q", 8 + oc, oc, os_, g2)

    with (
        nc.sbuf_tensor("wbufA0", [P, DT, 512], BF16) as wbA0,
        nc.sbuf_tensor("wbufA1", [P, DT, 512], BF16) as wbA1,
        nc.psum_tensor("tpA", [P, 8, P], BF16) as tpA,
        nc.psum_tensor("tpB", [P, 8, P], BF16) as tpB,
        nc.psum_tensor("tpC", [P, 8, P], BF16) as tpC,
        nc.psum_tensor("tpD", [P, 8, P], BF16) as tpD,
        nc.psum_tensor("pm0", [P, 512], F32) as pm0,
        nc.psum_tensor("pm1", [P, 512], F32) as pm1,
        nc.psum_tensor("pm2", [P, 512], F32) as pm2,
        nc.psum_tensor("pm3", [P, 512], F32) as pm3,
        nc.Block() as blk,
    ):
        wbuf = [wbA0, wbA1]
        tp = [[tpA, tpB], [tpC, tpD]]
        pmm = [pm0, pm1, pm2, pm3]

        @blk.sync
        def _(sy):
            sy.dma_start(out=X_res[:, :, :],
                         in_=_ap(x_own, 0, [[D, P], [P * D, NT], [1, D]])
                         ).then_inc(aX, 16)
            for bsrc, bdst in ((b_q, bq_sb), (b_k, bk_sb), (b_v, bv_sb),
                               (b_o, bo_sb)):
                sy.dma_start(out=bdst[:, :], in_=bsrc[:, :]).then_inc(aB, 16)
            sy.dma_start(out=wr_sb[:, :, :],
                         in_=_ap(wr, 0, [[E, P], [E * P, DT], [1, E]])
                         ).then_inc(aB, 16)
            for idx in range(96):
                kind, _, oc, a3, g2 = a_group(idx)
                sy.wait_ge(aPe, idx + 1)
                if kind in ("k", "v"):
                    dst = k_loc if kind == "k" else v_loc
                    out_ap = dst[a3 * P:(a3 + 1) * P,
                                 oc * 512:(oc + 1) * 512]
                else:
                    ot = oc * 4 + a3
                    out_ap = _ap(q_loc, ot * P * TOK + g2 * 512,
                                 [[TOK, P], [1, 512]])
                sy.dma_start(out=out_ap,
                             in_=kvst[idx % 4]).then_inc(aKVs[idx % 4], 16)
            sy.wait_ge(aB, 80)

        @blk.gpsimd
        def _(g):
            g = _SER(g)
            g.memset(ident[:, :], 0.0).then_inc(aId, 1)
            g.wait_ge(aId, 1)
            g.affine_select(out=ident[:, :], in_=ident[:, :],
                            compare_op=ALU.not_equal, fill=1.0, base=0,
                            pattern=[[-1, P]], channel_multiplier=1
                            ).then_inc(aId, 1)
            for i, src in enumerate(a_strips):
                if i == 8:
                    for q in range(4):
                        g.wait_ge(aKVs[q], 16 * 8)
                    g.collective_compute(
                        "AllGather", ALU.bypass,
                        replica_groups=[[0, 1, 2, 3], [4, 5, 6, 7]],
                        ins=[k_loc.ap().opt()], outs=[k_full.ap().opt()],
                    ).then_inc(ccK)
                if i >= 2:
                    g.wait_ge(aPe, (i - 2) * 8 + 8)
                g.dma_start(out=wbuf[i % 2][:, :, :], in_=src
                            ).then_inc(aWs[i % 2], 16)
            for q in range(4):
                g.wait_ge(aKVs[q], 16 * 16)
            g.collective_compute(
                "AllGather", ALU.bypass,
                replica_groups=[[0, 1, 2, 3], [4, 5, 6, 7]],
                ins=[v_loc.ap().opt()], outs=[v_full.ap().opt()],
            ).then_inc(ccV)

        @blk.scalar
        def _(s):
            s = _SER(s)
            s.wait_ge(aX, 16)
            for j in range(NT):
                s.activation(out=junk, in_=X_res[:, j, :], func=AF.Square,
                             accum_out=c_ssq[:, j:j + 1]).then_inc(aS1, 1)
            def a_evac(j):
                for half in range(2):
                    s.wait_ge(aTt, 2 * j + half + 1)
                    s.copy(
                        actT[:, half * 8:(half + 1) * 8, j * P:(j + 1) * P],
                        tp[j % 2][half][:, :, :]).then_inc(aEv, 1)
            for j in range(NT):
                s.wait_ge(aV2, j + 1)
                s.activation(out=c_rstd[:, j:j + 1], in_=c_t1[:, j:j + 1],
                             func=AF.Sqrt).then_inc(aS2, 1)
                if j >= 2:
                    a_evac(j - 2)
            a_evac(NT - 2)
            a_evac(NT - 1)

        @blk.vector
        def _(v):
            v = _SER(v)
            v.wait_ge(aX, 16)
            for j in range(NT):
                v.tensor_reduce(c_sum[:, j:j + 1], X_res[:, j, :], AX.X,
                                ALU.add)
                v.wait_ge(aS1, j + 1)
                v.tensor_scalar(c_t0[:, j:j + 1], c_sum[:, j:j + 1],
                                1.0 / D, None, ALU.mult)
                v.tensor_scalar(c_negm[:, j:j + 1], c_t0[:, j:j + 1],
                                -1.0, None, ALU.mult)
                v.tensor_scalar(c_t1[:, j:j + 1], c_ssq[:, j:j + 1],
                                1.0 / D, None, ALU.mult)
                v.scalar_tensor_tensor(c_t1[:, j:j + 1], c_t0[:, j:j + 1],
                                       c_negm[:, j:j + 1], c_t1[:, j:j + 1],
                                       ALU.mult, ALU.add)
                v.tensor_scalar(c_t1[:, j:j + 1], c_t1[:, j:j + 1],
                                EPS, None, ALU.add)
                v.reciprocal(c_t1[:, j:j + 1], c_t1[:, j:j + 1]
                             ).then_inc(aV2, 1)
                v.wait_ge(aS2, j + 1)
                if j >= 2:
                    v.wait_ge(aTt, 2 * (j - 2) + 2)
                v.tensor_scalar(xst[j % 2], X_res[:, j, :],
                                c_negm[:, j:j + 1], c_rstd[:, j:j + 1],
                                ALU.add, ALU.mult).then_inc(aV3, 1)
            for idx in range(NG):
                kind, _, oc, a3, g2 = a_group(idx)
                v.wait_ge(aPm, idx + 1)
                b = idx % 4
                if kind in ("k", "v"):
                    if idx >= 4:
                        v.wait_ge(aKVs[idx % 4], 16 * (idx // 4))
                    v.tensor_copy(kvst[idx % 4], pmm[b][:, :]).then_inc(aPe, 1)
                else:
                    ot = oc * 4 + a3
                    if idx == 64:
                        v.wait_ge(aB, 80)
                    if idx >= 4:
                        v.wait_ge(aKVs[idx % 4], 16 * (idx // 4))
                    v.tensor_scalar(kvst[idx % 4],
                                    pmm[b][:, :], bq_sb[:, ot:ot + 1],
                                    INV_C, ALU.add, ALU.mult).then_inc(aPe, 1)

        @blk.tensor
        def _(t):
            t.wait_ge(aId, 2)
            for j in range(NT):
                t.wait_ge(aV3, j + 1)
                if j >= 2:
                    t.wait_ge(aEv, 2 * (j - 2) + 2)
                for dt in range(DT):
                    bank = tp[j % 2][dt // 8]
                    ins = t.transpose(bank[:, dt % 8, :],
                                      xst[j % 2][:, dt * P:(dt + 1) * P],
                                      ident[:, :])
                    if dt % 8 == 7:
                        ins.then_inc(aTt, 1)
            for idx in range(NG):
                kind, strip, oc, a3, g2 = a_group(idx)
                t.wait_ge(aWs[strip % 2], 16 * (strip // 2 + 1))
                if kind in ("k", "v"):
                    t.wait_ge(aEv, 2 * a3 + 2)
                else:
                    t.wait_ge(aEv, 2 * NT)
                if idx >= 4:
                    t.wait_ge(aPe, idx - 3)
                b = idx % 4
                for dt in range(DT):
                    if kind in ("k", "v"):
                        ins = t.matmul(pmm[b][:, :],
                                       actT[:, dt, a3 * P:(a3 + 1) * P],
                                       wbuf[strip % 2][:, dt, :],
                                       start=(dt == 0), stop=(dt == DT - 1))
                    else:
                        ins = t.matmul(
                            pmm[b][:, :],
                            wbuf[strip % 2][:, dt, a3 * P:(a3 + 1) * P],
                            actT[:, dt, g2 * 512:(g2 + 1) * 512],
                            start=(dt == 0), stop=(dt == DT - 1))
                ins.then_inc(aPm, 1)

    if {"dbg_kv", "dbg_q", "dbg_lnt"} & set(dbg):
        dS = sem("dS")
        with nc.Block() as blk:
            @blk.sync
            def _(sy):
                n = 0
                for key, src in (("dbg_kv", None),
                                 ("dbg_lnt", actT[:, :, :])):
                    if key == "dbg_kv" and key in dbg:
                        sy.dma_start(out=dbg[key].ap()[:, 0:D], in_=k_loc.ap()
                                     ).then_inc(dS, 16)
                        sy.dma_start(out=dbg[key].ap()[:, D:2 * D],
                                     in_=v_loc.ap()).then_inc(dS, 16)
                        n += 32
                        continue
                    if key in dbg:
                        sy.dma_start(out=dbg[key].ap(), in_=src
                                     ).then_inc(dS, 16)
                        n += 16
                sy.wait_ge(dS, n)

    if _STOP <= "B":
        emit_G()
        es.close()
        return nc, dbg
    cKHs = [sem("cKH0"), sem("cKH1")]
    cKb = sem("cKb")
    cVHs = [sem("cVH0"), sem("cVH1")]
    cSm = sem("cSm")
    cMx = sem("cMx")
    cMl = sem("cMl")
    cEx = sem("cEx")
    cFa = sem("cFa")
    cNm = sem("cNm")
    cPTs = [sem(f"cPT{i}") for i in range(4)]
    cPV = sem("cPV")
    cCx = sem("cCx")

    cQHs = [sem("cQH0"), sem("cQH1")]
    with (
        nc.sbuf_tensor("Q_h", [P, 2, TOK], BF16) as Q_h,
        nc.sbuf_tensor("K_h", [P, 2, S], BF16) as K_h,
        nc.sbuf_tensor("V_h", [P, 2, NKT, P], BF16) as V_h,
        nc.sbuf_tensor("P_buf", [P, 2, S], BF16) as P_buf,
        nc.sbuf_tensor("negm8", [P, 2, 8], F32) as negm8,
        nc.sbuf_tensor("sum8", [P, 2, 8], F32) as sum8,
        nc.sbuf_tensor("fac8", [P, 2, 8], F32) as fac8,
        nc.sbuf_tensor("c_mall", [P, 2], F32) as c_mall,
        nc.sbuf_tensor("c_sall", [P, 2], F32) as c_sall,
        nc.sbuf_tensor("junk8", [P, 8], F32) as junk8,
        nc.psum_tensor("ps_s", [P, 4, 512], F32) as ps_s,
        nc.psum_tensor("ps_c0", [P, 512], F32) as ps_c0,
        nc.psum_tensor("ps_c1", [P, 512], F32) as ps_c1,
        nc.Block() as blk,
    ):
        ps_c = [ps_c0, ps_c1]

        @blk.sync
        def _(sy):
            def k_load(h):
                if h >= H:
                    return
                if h == 0:
                    sy.wait_ge(ccK, 1)
                if h >= 2:
                    sy.wait_ge(cSm, 64 * (h - 1))
                sy.dma_start_transpose(
                    out=K_h[:, h % 2, :],
                    in_=k_full[:, h * P:(h + 1) * P]).then_inc(cKHs[h % 2], 16)

            for q in range(4):
                sy.wait_ge(aKVs[q], 16 * 24)
            k_load(0)
            k_load(1)
            for gq in range(H * NT):
                sl, gb = gq % 4, gq // 4
                sy.wait_ge(cNm, gq * 8 + 8)
                sy.wait_ge(cPV, gb)
                sy.dma_start_transpose(
                    out=PT4[:, :, sl * P:(sl + 1) * P],
                    in_=P_buf[:, gq % 2, :].rearrange(
                        "q (kt p) -> q kt p", kt=NKT)
                    ).then_inc(cPTs[sl], 16)
                if gq % 8 == 5:
                    k_load(gq // 8 + 2)

        @blk.scalar
        def _(s):
            s = _SER(s)
            for q in range(4):
                s.wait_ge(aKVs[q], 16 * 24)
            for h in range(H):
                if h >= 2:
                    s.wait_ge(cSm, 64 * (h - 1))
                s.dma_start(out=Q_h[:, h % 2, :],
                            in_=q_loc[h * P:(h + 1) * P, :]
                            ).then_inc(cQHs[h % 2], 16)
                if h == 0:
                    s.wait_ge(ccV, 1)
                if h >= 2:
                    s.wait_ge(cPV, 2 * h - 2)
                s.dma_start(
                    out=V_h[:, h % 2, :, :],
                    in_=_ap(v_full, h * P, [[D, P], [P * D, NKT], [1, P]])
                    ).then_inc(cVHs[h % 2], 16)
                for jq in range(NT):
                    gq = h * NT + jq
                    jr = jq % 2
                    if gq >= 2:
                        psl = (gq - 2) % 4
                        s.wait_ge(cPTs[psl], 16 * ((gq - 2) // 4 + 1))
                    if gq >= 2:
                        s.wait_ge(cFa, gq - 1)
                    for kc in range(8):
                        gc = gq * 8 + kc
                        if kc % 2 == 0:
                            s.wait_ge(cMx, gc // 2 + 1)
                        s.activation(out=P_buf[:, gq % 2,
                                                kc * 512:(kc + 1) * 512],
                                     in_=ps_s[:, kc % 4, :], func=AF.Exp,
                                     bias=negm8[:, jr, kc:kc + 1], scale=1.0,
                                     accum_out=sum8[:, jr, kc:kc + 1]
                                     ).then_inc(cEx, 1)
                    s.wait_ge(cMl, gq + 1)
                    s.activation(out=fac8[:, jr, :], in_=negm8[:, jr, :],
                                 func=AF.Exp, bias=c_mall[:, jr:jr + 1],
                                 scale=-1.0).then_inc(cEx, 1)

        @blk.vector
        def _(v):
            v = _SER(v)

            def v_ctx_evac(kb):
                v.wait_ge(cPV, kb + 1)
                v.tensor_scalar(actT[:, kb // 2,
                                     (kb % 2) * 512:(kb % 2) * 512 + 512],
                                ps_c[kb % 2][:, :],
                                bv_sb[:, kb // 2:kb // 2 + 1],
                                None, ALU.add).then_inc(cCx, 1)

            def v_stats_norm(gq):
                jr = (gq % NT) % 2
                v.wait_ge(cEx, gq * 9 + 9)
                v.tensor_mul(junk8[:, :], sum8[:, jr, :], fac8[:, jr, :])
                v.tensor_reduce(c_sall[:, jr:jr + 1], junk8[:, :],
                                AX.X, ALU.add)
                v.reciprocal(c_sall[:, jr:jr + 1], c_sall[:, jr:jr + 1])
                v.tensor_scalar(fac8[:, jr, :], fac8[:, jr, :],
                                c_sall[:, jr:jr + 1], None, ALU.mult
                                ).then_inc(cFa, 1)
                for kc in range(8):
                    v.tensor_scalar(P_buf[:, gq % 2,
                                          kc * 512:(kc + 1) * 512],
                                    P_buf[:, gq % 2,
                                          kc * 512:(kc + 1) * 512],
                                    fac8[:, jr, kc:kc + 1], None,
                                    ALU.mult).then_inc(cNm, 1)

            prev = None
            for h in range(H):
                v.wait_ge(cKHs[h % 2], 16 * (h // 2 + 1))
                if h == 0:
                    v.wait_ge(aB, 80)
                v.tensor_scalar(K_h[:, h % 2, :], K_h[:, h % 2, :],
                                bk_sb[:, h:h + 1], None, ALU.add
                                ).then_inc(cKb, 1)
                for jq in range(NT):
                    gq = h * NT + jq
                    jr = jq % 2
                    for kp in range(4):
                        v.wait_ge(cSm, gq * 8 + 2 * kp + 2)
                        v.tensor_reduce(negm8[:, jr, 2 * kp:2 * kp + 2],
                                        ps_s[:, 2 * (kp % 2):2 * (kp % 2) + 2, :],
                                        AX.X, ALU.max,
                                        negate=True).then_inc(cMx, 1)
                    v.tensor_reduce(c_mall[:, jr:jr + 1], negm8[:, jr, :],
                                    AX.X, ALU.min).then_inc(cMl, 1)
                    if prev is not None:
                        v_stats_norm(prev)
                    prev = gq
                    if gq >= 5 and (gq - 5) % 4 == 0:
                        v_ctx_evac((gq - 5) // 4)
            v_stats_norm(prev)
            v_ctx_evac(30)
            v_ctx_evac(31)

        @blk.tensor
        def _(t):
            def t_pv(gb):
                hh = gb // 2
                for q in range(4):
                    t.wait_ge(cPTs[q], 16 * (gb + 1))
                if gb >= 2:
                    t.wait_ge(cCx, gb - 1)
                for kt in range(NKT):
                    ins = t.matmul(ps_c[gb % 2][:, :],
                                   V_h[:, hh % 2, kt, :],
                                   PT4[:, kt, :],
                                   start=(kt == 0),
                                   stop=(kt == NKT - 1))
                ins.then_inc(cPV, 1)

            for h in range(H):
                t.wait_ge(cKb, h + 1)
                t.wait_ge(cQHs[h % 2], 16 * (h // 2 + 1))
                t.wait_ge(cVHs[h % 2], 16 * (h // 2 + 1))
                for jq in range(NT):
                    gq = h * NT + jq
                    for kc in range(8):
                        n = gq * 8 + kc
                        if n >= 4 and kc % 2 == 0:
                            pg, pk = divmod(n - 3, 8)
                            t.wait_ge(cEx, pg * 9 + pk + 1)
                        t.matmul(ps_s[:, kc % 4, :],
                                 Q_h[:, h % 2, jq * P:(jq + 1) * P],
                                 K_h[:, h % 2, kc * 512:(kc + 1) * 512],
                                 start=True, stop=True).then_inc(cSm, 1)
                    if gq % 4 == 0 and gq >= 4:
                        t_pv(gq // 4 - 1)
            t_pv(31)


    if "dbg_ctx" in dbg:
        dC = sem("dC")
        with nc.Block() as blk:
            @blk.sync
            def _(sy):
                sy.dma_start(out=dbg["dbg_ctx"].ap(), in_=actT[:, :, :]
                             ).then_inc(dC, 16)
                sy.wait_ge(dC, 16)

    if _STOP <= "C":
        emit_G()
        es.close()
        return nc, dbg

    dWs = [sem("dW0"), sem("dW1")]
    dMM = sem("dMM")
    dEv = sem("dEv")
    dTp = sem("dTp")
    dAdd = sem("dAdd")

    with (
        nc.sbuf_tensor("wbufD0", [P, DT, 512], BF16) as wbD0,
        nc.sbuf_tensor("wbufD1", [P, DT, 512], BF16) as wbD1,
        nc.sbuf_tensor("stageO", [P, 2, 512], BF16) as stageO,
        nc.psum_tensor("pd0", [P, 512], F32) as pd0,
        nc.psum_tensor("pd1", [P, 512], F32) as pd1,
        nc.psum_tensor("ptd0", [P, 4, P], BF16) as ptd0,
        nc.psum_tensor("ptd1", [P, 4, P], BF16) as ptd1,
        nc.Block() as blk,
    ):
        wbufD = [wbD0, wbD1]
        pd = [pd0, pd1]
        ptd = [ptd0, ptd1]

        def d_tp(t, pidx):
            t.wait_ge(dEv, pidx + 1)
            if pidx >= 2:
                t.wait_ge(dAdd, pidx - 1)
            for k in range(4):
                ins = t.transpose(ptd[pidx % 2][:, k, :],
                                  stageO[:, pidx % 2, k * P:(k + 1) * P],
                                  ident[:, :])
            ins.then_inc(dTp, 1)

        def d_add(v, pidx):
            poc, pr = divmod(pidx, 8)
            pos, pg2 = divmod(pr, 2)
            pot = poc * 4 + pos
            v.wait_ge(dTp, pidx + 1)
            v.tensor_tensor(
                X_res[:, pg2 * 4:pg2 * 4 + 4, pot * P:(pot + 1) * P],
                ptd[pidx % 2][:, :, :],
                X_res[:, pg2 * 4:pg2 * 4 + 4, pot * P:(pot + 1) * P],
                ALU.add).then_inc(dAdd, 1)

        @blk.gpsimd
        def _(g):
            g = _SER(g)
            for oc in range(4):
                if oc >= 2:
                    g.wait_ge(dEv, (oc - 2) * 8 + 8)
                g.dma_start(out=wbufD[oc % 2][:, :, :],
                            in_=strip_src(woT, oc)).then_inc(dWs[oc % 2], 16)

        @blk.tensor
        def _(t):
            for idx in range(32):
                oc, r = divmod(idx, 8)
                os_, g2 = divmod(r, 2)
                t.wait_ge(dWs[oc % 2], 16 * (oc // 2 + 1))
                if idx >= 2:
                    t.wait_ge(dEv, idx - 1)
                for ct in range(DT):
                    ins = t.matmul(pd[idx % 2][:, :],
                                   wbufD[oc % 2][:, ct, os_ * P:(os_ + 1) * P],
                                   actT[:, ct, g2 * 512:(g2 + 1) * 512],
                                   start=(ct == 0), stop=(ct == DT - 1))
                ins.then_inc(dMM, 1)
                if idx >= 1:
                    d_tp(t, idx - 1)
            d_tp(t, 31)

        @blk.vector
        def _(v):
            v = _SER(v)
            for idx in range(32):
                oc, r = divmod(idx, 8)
                os_, g2 = divmod(r, 2)
                ot = oc * 4 + os_
                v.wait_ge(dMM, idx + 1)
                if idx >= 2:
                    v.wait_ge(dTp, idx - 1)
                v.tensor_scalar(stageO[:, idx % 2, :], pd[idx % 2][:, :],
                                bo_sb[:, ot:ot + 1], None, ALU.add
                                ).then_inc(dEv, 1)
                if idx >= 1:
                    d_add(v, idx - 1)
            d_add(v, 31)

    if "dbg_x1" in dbg:
        dX = sem("dX")
        with nc.Block() as blk:
            @blk.sync
            def _(sy):
                sy.dma_start(out=dbg["dbg_x1"].ap(), in_=X_res[:, :, :]
                             ).then_inc(dX, 16)
                sy.wait_ge(dX, 16)

    if _STOP <= "D":
        emit_G()
        es.close()
        return nc, dbg

    eS1 = sem("eS1")
    eV2 = sem("eV2")
    eS2 = sem("eS2")
    eV3 = sem("eV3")
    eTt = sem("eTt")
    eEv = sem("eEv")
    eRm = sem("eRm")
    eRe = sem("eRe")
    eRx = sem("eRx")
    eRw = sem("eRw")

    with (
        nc.sbuf_tensor("stgf", [P, 2, D], F32) as stgf,
        nc.sbuf_tensor("stghi", [P, 2, D], BF16) as stghi,
        nc.sbuf_tensor("stglo", [P, 2, D], BF16) as stglo,
        nc.sbuf_tensor("junkE", [P, D], BF16) as junkE,
        nc.sbuf_tensor("lg_sb", [P, NT, E], F32) as lg_sb,
        nc.sbuf_tensor("e8", [P, E], F32) as e8,
        nc.sbuf_tensor("r_m1", [P, 1], F32) as r_m1,
        nc.sbuf_tensor("r_nm1", [P, 1], F32) as r_nm1,
        nc.sbuf_tensor("r_m2", [P, 1], F32) as r_m2,
        nc.sbuf_tensor("r_msk", [P, E], F32) as r_msk,
        nc.sbuf_tensor("r_sel", [P, E], F32) as r_sel,
        nc.sbuf_tensor("r_wun", [P, E], F32) as r_wun,
        nc.sbuf_tensor("r_ss", [P, 1], F32) as r_ss,
        nc.psum_tensor("tpE0", [P, 8, P], BF16) as tpE0,
        nc.psum_tensor("tpE1", [P, 8, P], BF16) as tpE1,
        nc.psum_tensor("tpE2", [P, 8, P], BF16) as tpE2,
        nc.psum_tensor("tpE3", [P, 8, P], BF16) as tpE3,
        nc.psum_tensor("prt", [P, E], F32) as prt,
        nc.Block() as blk,
    ):
        tpE = [tpE0, tpE1, tpE2, tpE3]

        @blk.scalar
        def _(s):
            s = _SER(s)
            for j in range(NT):
                s.activation(out=junkE[:, :], in_=X_res[:, j, :], func=AF.Square,
                             accum_out=c_ssq[:, j:j + 1]).then_inc(eS1, 1)
            def e_evac(j):
                for q in range(4):
                    s.wait_ge(eTt, 4 * j + q + 1)
                    dst = actT if q < 2 else hlT
                    half = q % 2
                    s.copy(
                        dst[:, half * 8:(half + 1) * 8, j * P:(j + 1) * P],
                        tpE[q][:, :, :]).then_inc(eEv, 1)
            for j in range(NT):
                s.wait_ge(eV2, j + 1)
                s.activation(out=c_rstd[:, j:j + 1], in_=c_t1[:, j:j + 1],
                             func=AF.Sqrt).then_inc(eS2, 1)
                if j >= 2:
                    e_evac(j - 2)
            e_evac(NT - 2)
            e_evac(NT - 1)
            for j in range(NT):
                s.wait_ge(eRe, 3 * j + 3)
                s.activation(out=e8[:, :], in_=lg_sb[:, j, :], func=AF.Exp,
                             bias=r_nm1[:, :], scale=1.0).then_inc(eRx, 1)

        @blk.vector
        def _(v):
            v = _SER(v)
            for j in range(NT):
                v.tensor_reduce(c_sum[:, j:j + 1], X_res[:, j, :], AX.X,
                                ALU.add)
                v.wait_ge(eS1, j + 1)
                v.tensor_scalar(c_t0[:, j:j + 1], c_sum[:, j:j + 1],
                                1.0 / D, None, ALU.mult)
                v.tensor_scalar(c_negm[:, j:j + 1], c_t0[:, j:j + 1],
                                -1.0, None, ALU.mult)
                v.tensor_scalar(c_t1[:, j:j + 1], c_ssq[:, j:j + 1],
                                1.0 / D, None, ALU.mult)
                v.scalar_tensor_tensor(c_t1[:, j:j + 1], c_t0[:, j:j + 1],
                                       c_negm[:, j:j + 1], c_t1[:, j:j + 1],
                                       ALU.mult, ALU.add)
                v.tensor_scalar(c_t1[:, j:j + 1], c_t1[:, j:j + 1],
                                EPS, None, ALU.add)
                v.reciprocal(c_t1[:, j:j + 1], c_t1[:, j:j + 1]
                             ).then_inc(eV2, 1)
                v.wait_ge(eS2, j + 1)
                if j >= 2:
                    v.wait_ge(eTt, 4 * (j - 2) + 4)
                v.tensor_scalar(stgf[:, j % 2, :], X_res[:, j, :],
                                c_negm[:, j:j + 1], c_rstd[:, j:j + 1],
                                ALU.add, ALU.mult)
                v.tensor_copy(stghi[:, j % 2, :], stgf[:, j % 2, :]
                              ).then_inc(eV3, 1)
                v.tensor_sub(stglo[:, j % 2, :], stgf[:, j % 2, :],
                             stghi[:, j % 2, :]).then_inc(eV3, 1)
            v.tensor_copy(wr_hi[:, :, :], wr_sb[:, :, :])
            v.tensor_sub(wr_lo[:, :, :], wr_sb[:, :, :], wr_hi[:, :, :]
                         ).then_inc(eV3, 1)
            for j in range(NT):
                v.wait_ge(eRm, j + 1)
                v.tensor_copy(lg_sb[:, j, :], prt[:, :]).then_inc(eRe, 1)
                v.tensor_reduce(r_m1[:, :], lg_sb[:, j, :], AX.X, ALU.max)
                if j >= 1:
                    v.wait_ge(eRx, j)
                v.tensor_scalar(r_nm1[:, :], r_m1[:, :], -1.0, None,
                                ALU.mult).then_inc(eRe, 1)
                v.tensor_scalar(r_msk[:, :], lg_sb[:, j, :], r_m1[:, :],
                                -1e30, ALU.is_ge, ALU.mult)
                v.tensor_add(r_msk[:, :], lg_sb[:, j, :], r_msk[:, :])
                v.tensor_reduce(r_m2[:, :], r_msk[:, :], AX.X, ALU.max
                                ).then_inc(eRe, 1)
                v.wait_ge(eRx, j + 1)
                v.tensor_scalar(r_sel[:, :], lg_sb[:, j, :], r_m2[:, :],
                                None, ALU.is_ge)
                v.tensor_mul(r_wun[:, :], e8[:, :], r_sel[:, :])
                v.tensor_reduce(r_ss[:, :], r_wun[:, :], AX.X, ALU.add)
                v.reciprocal(r_ss[:, :], r_ss[:, :])
                v.tensor_scalar(W_sel[:, :, j], r_wun[:, :], r_ss[:, :],
                                None, ALU.mult).then_inc(eRw, 1)

        @blk.tensor
        def _(t):
            for j in range(NT):
                for pas in range(2):
                    t.wait_ge(eV3, 2 * j + pas + 1)
                    if j >= 1:
                        t.wait_ge(eEv, 4 * (j - 1) + 2 * pas + 2)
                    src = stghi if pas == 0 else stglo
                    for dt in range(DT):
                        bank = tpE[2 * pas + dt // 8]
                        ins = t.transpose(bank[:, dt % 8, :],
                                          src[:, j % 2, dt * P:(dt + 1) * P],
                                          ident[:, :])
                        if dt % 8 == 7:
                            ins.then_inc(eTt, 1)
            t.wait_ge(eV3, 2 * NT + 1)
            t.wait_ge(eEv, 4 * NT)
            for j in range(NT):
                if j >= 1:
                    t.wait_ge(eRe, 3 * (j - 1) + 1)
                nmm = 0
                for lhs_src, rhs_src in ((actT, wr_hi), (actT, wr_lo),
                                         (hlT, wr_hi)):
                    for dt in range(DT):
                        ins = t.matmul(prt[:, :],
                                       lhs_src[:, dt, j * P:(j + 1) * P],
                                       rhs_src[:, dt, :],
                                       start=(nmm == 0), stop=(nmm == 47))
                        nmm += 1
                ins.then_inc(eRm, 1)

    if {"dbg_w8", "dbg_ln2"} & set(dbg):
        dW8 = sem("dW8")
        with nc.Block() as blk:
            @blk.sync
            def _(sy):
                n = 0
                for key, src in (("dbg_w8", W_sel[:, :, :]),
                                 ("dbg_ln2", actT[:, :, :])):
                    if key in dbg:
                        sy.dma_start(out=dbg[key].ap(), in_=src
                                     ).then_inc(dW8, 16)
                        n += 16
                sy.wait_ge(dW8, n)

    if _STOP <= "E":
        emit_G()
        es.close()
        return nc, dbg

    fWs = [sem("fW0"), sem("fW1")]
    fMM = sem("fMM")
    fEvG = sem("fEvG")
    fEvV = sem("fEvV")
    fTp = sem("fTp")
    fAdd = sem("fAdd")

    f_strips = []
    for i in range(E):
        for fc in range(4):
            f_strips.append(strip_src(wg, fc, i))
            f_strips.append(strip_src(wu, fc, i))
        for oc in range(4):
            f_strips.append(strip_src(wdn, oc, i))

    fgroups = []
    for i in range(E):
        for fc in range(4):
            for fs in range(4):
                for g2 in range(2):
                    fgroups.append(("g", i, fc, fs, g2, i * 12 + 2 * fc))
            for fs in range(4):
                for g2 in range(2):
                    fgroups.append(("u", i, fc, fs, g2, i * 12 + 2 * fc + 1))
        for oc in range(4):
            for os_ in range(4):
                for g2 in range(2):
                    fgroups.append(("d", i, oc, os_, g2, i * 12 + 8 + oc))
    NFG = len(fgroups)
    evac_of = []
    nG = nV = 0
    d_index = {}
    g_evac_idx = {}
    u_evac_idx = {}
    ndi = 0
    for m, (kind, i, c, s_, g2, si) in enumerate(fgroups):
        if kind == "g":
            nG += 1
            evac_of.append(("G", nG))
            g_evac_idx[(i, c, s_, g2)] = nG
        else:
            nV += 1
            evac_of.append(("V", nV))
            if kind == "u":
                u_evac_idx[(i, c, s_, g2)] = nV
            else:
                d_index[m] = ndi
                ndi += 1
    NDG = ndi
    d_list = [m for m in range(NFG) if fgroups[m][0] == "d"]
    strip_last = {}
    for m, grp in enumerate(fgroups):
        strip_last[grp[5]] = m

    with (
        nc.sbuf_tensor("wbufF0", [P, DT, 512], BF16) as wbF0,
        nc.sbuf_tensor("wbufF1", [P, DT, 512], BF16) as wbF1,
        nc.sbuf_tensor("sg_buf", [P, 4, TOK], BF16) as sg_buf,
        nc.sbuf_tensor("stageE", [P, 2, 512], BF16) as stageE,
        nc.psum_tensor("pf0", [P, 512], F32) as pf0,
        nc.psum_tensor("pf1", [P, 512], F32) as pf1,
        nc.psum_tensor("pf2", [P, 512], F32) as pf2,
        nc.psum_tensor("pf3", [P, 512], F32) as pf3,
        nc.psum_tensor("ptf0", [P, 4, P], BF16) as ptf0,
        nc.psum_tensor("ptf1", [P, 4, P], BF16) as ptf1,
        nc.Block() as blk,
    ):
        wbufF = [wbF0, wbF1]
        pf = [pf0, pf1, pf2, pf3]
        ptf = [ptf0, ptf1]

        def f_tp(t, di):
            m = d_list[di]
            _, ei = evac_of[m]
            t.wait_ge(fEvV, ei)
            if di >= 2:
                t.wait_ge(fAdd, 4 * (di - 2) + 4)
            for k in range(4):
                ins = t.transpose(ptf[di % 2][:, k, :],
                                  stageE[:, di % 2, k * P:(k + 1) * P],
                                  ident[:, :])
            ins.then_inc(fTp, 1)

        def f_add(v, di):
            m = d_list[di]
            _, i, oc, os_, g2, _ = fgroups[m]
            ot = oc * 4 + os_
            v.wait_ge(fTp, di + 1)
            for k in range(4):
                j = g2 * 4 + k
                v.scalar_tensor_tensor(
                    X_res[:, j, ot * P:(ot + 1) * P],
                    ptf[di % 2][:, k, :], W_sel[:, i, j:j + 1],
                    X_res[:, j, ot * P:(ot + 1) * P],
                    ALU.mult, ALU.add).then_inc(fAdd, 1)

        @blk.gpsimd
        def _(g):
            g = _SER(g)
            for si, src in enumerate(f_strips):
                if si >= 2:
                    ek, ei = evac_of[strip_last[si - 2]]
                    g.wait_ge(fEvG if ek == "G" else fEvV, ei)
                g.dma_start(out=wbufF[si % 2][:, :, :], in_=src
                            ).then_inc(fWs[si % 2], 16)

        @blk.tensor
        def _(t):
            for m, (kind, i, c, s_, g2, si) in enumerate(fgroups):
                t.wait_ge(fWs[si % 2], 16 * (si // 2 + 1))
                if m >= 4:
                    ek, ei = evac_of[m - 4]
                    t.wait_ge(fEvG if ek == "G" else fEvV, ei)
                if kind == "d" and c == 0 and s_ == 0 and g2 == 0:
                    t.wait_ge(fEvV, u_evac_idx[(i, 3, 3, 1)])
                b = m % 4
                rhs_base = actT if kind in ("g", "u") else hlT
                for dt in range(DT):
                    ins = t.matmul(pf[b][:, :],
                                   wbufF[si % 2][:, dt, s_ * P:(s_ + 1) * P],
                                   rhs_base[:, dt, g2 * 512:(g2 + 1) * 512],
                                   start=(dt == 0), stop=(dt == DT - 1))
                ins.then_inc(fMM, 1)
                if kind == "d":
                    di = d_index[m]
                    if di >= 1:
                        f_tp(t, di - 1)
            f_tp(t, NDG - 1)

        @blk.scalar
        def _(s):
            s = _SER(s)
            for m, (kind, i, c, s_, g2, si) in enumerate(fgroups):
                if kind != "g":
                    continue
                s.wait_ge(fMM, m + 1)
                if c >= 1:
                    s.wait_ge(fEvV, u_evac_idx[(i, c - 1, s_, g2)])
                elif i >= 1:
                    s.wait_ge(fEvV, u_evac_idx[(i - 1, 3, s_, g2)])
                s.activation(out=sg_buf[:, s_, g2 * 512:(g2 + 1) * 512],
                             in_=pf[m % 4][:, :], func=AF.Silu
                             ).then_inc(fEvG, 1)

        @blk.vector
        def _(v):
            v = _SER(v)
            for m, (kind, i, c, s_, g2, si) in enumerate(fgroups):
                if kind == "g":
                    continue
                v.wait_ge(fMM, m + 1)
                if kind == "u":
                    v.wait_ge(fEvG, g_evac_idx[(i, c, s_, g2)])
                    v.tensor_tensor(
                        hlT[:, c * 4 + s_, g2 * 512:(g2 + 1) * 512],
                        sg_buf[:, s_, g2 * 512:(g2 + 1) * 512],
                        pf[m % 4][:, :], ALU.mult).then_inc(fEvV, 1)
                else:
                    di = d_index[m]
                    if di >= 2:
                        v.wait_ge(fTp, di - 1)
                    v.tensor_copy(stageE[:, di % 2, :], pf[m % 4][:, :]
                                  ).then_inc(fEvV, 1)
                    if di >= 1:
                        f_add(v, di - 1)
            f_add(v, NDG - 1)

    emit_G()
    es.close()
    return nc, dbg


_cache = {}


def _get_nc():
    key = (tuple(sorted(_DBG)), _STOP)
    if key not in _cache:
        _cache[key] = build_nc()
    return _cache[key]


def kernel(L0_attn_x, L0_attn_wq, L0_attn_bq, L0_attn_wk, L0_attn_bk,
           L0_attn_wv, L0_attn_bv, L0_attn_wo, L0_attn_bo,
           L0_ffn_Wr, L0_ffn_Wg, L0_ffn_Wu, L0_ffn_Wd, _extra_out=None):
    x = np.asarray(L0_attn_x, dtype=np.float32)
    shared = {
        "wqT": np.ascontiguousarray(np.asarray(L0_attn_wq, np.float32).T),
        "wkT": np.ascontiguousarray(np.asarray(L0_attn_wk, np.float32).T),
        "wvT": np.ascontiguousarray(np.asarray(L0_attn_wv, np.float32).T),
        "woT": np.ascontiguousarray(np.asarray(L0_attn_wo, np.float32).T),
        "b_q": np.ascontiguousarray(
            np.asarray(L0_attn_bq, np.float32).reshape(DT, P).T),
        "b_k": np.ascontiguousarray(
            np.asarray(L0_attn_bk, np.float32).reshape(DT, P).T),
        "b_v": np.ascontiguousarray(
            np.asarray(L0_attn_bv, np.float32).reshape(DT, P).T),
        "b_o": np.ascontiguousarray(
            np.asarray(L0_attn_bo, np.float32).reshape(DT, P).T),
        "wr": np.ascontiguousarray(np.asarray(L0_ffn_Wr, np.float32)),
        "wg": np.ascontiguousarray(np.asarray(L0_ffn_Wg, np.float32)),
        "wu": np.ascontiguousarray(np.asarray(L0_ffn_Wu, np.float32)),
        "wdn": np.ascontiguousarray(np.asarray(L0_ffn_Wd, np.float32)),
    }
    nc, dbg = _get_nc()
    in_maps = []
    for c in range(8):
        b, qb = divmod(c, 4)
        m = dict(shared)
        m["x_own"] = np.ascontiguousarray(x[b, qb * TOK:(qb + 1) * TOK])
        in_maps.append(m)
    res = run_bass_kernel_spmd(nc, in_maps, core_ids=list(range(8)))
    outf = np.empty((2, S, D), np.float32)
    for c in range(8):
        b, qb = divmod(c, 4)
        outf[b, qb * TOK:(qb + 1) * TOK] = res.results[c]["out"]
    if _extra_out is not None:
        for c in range(8):
            _extra_out.append({k: np.asarray(res.results[c][k]) for k in dbg})
    return outf


# version 09
# speedup vs baseline: 1.0313x; 1.0313x over previous
import math
from contextlib import ExitStack

import numpy as np

import concourse.bass as bass
import concourse.mybir as mybir
from concourse.bass_utils import run_bass_kernel_spmd

F32 = mybir.dt.float32
BF16 = mybir.dt.bfloat16
AF = mybir.ActivationFunctionType
ALU = mybir.AluOpType
AX = mybir.AxisListType

P = 128
D = 2048
TOK = 1024
S = 4096
NT = TOK // P
DT = D // P
H = 16
NKT = S // P
E = 8
INV_C = 1.0 / math.sqrt(128.0)
EPS = 1e-5

_DBG = set()
_STOP = "Z"


def _ap(t, offset, dims):
    return bass.AP(t, offset, dims)


def _ap_range(a):
    lo = a.offset
    ext = 1
    dims = list(a.ap)
    for step, count in dims[1:]:
        ext += abs(step) * (count - 1)
    return (a.tensor.name, lo, lo + ext)


def _overlaps(x, y):
    return x[0] == y[0] and x[1] < y[2] and y[1] < x[2]


class _SER:

    _WR = {"out", "accum_out"}
    _PASS = {"wait_ge", "drain", "nop", "sem_inc", "dma_start",
             "dma_start_transpose", "collective_compute", "wait_op",
             "memzero"}

    def __init__(self, e):
        self._e = e
        self._dw = []
        self._dr = []

    def __getattr__(self, name):
        f = getattr(self._e, name)
        if name in self._PASS or not callable(f):
            return f

        def wrapped(*args, **kw):
            ws, rs = [], []
            for i, a in enumerate(args):
                if isinstance(a, bass.AP):
                    (ws if i == 0 else rs).append(_ap_range(a))
            for k, a in kw.items():
                if isinstance(a, bass.AP):
                    (ws if k in self._WR else rs).append(_ap_range(a))
            conflict = any(_overlaps(t, w) for t in ws + rs
                           for w in self._dw) or                        any(_overlaps(w, r) for w in ws for r in self._dr)
            if conflict:
                self._e.drain()
                self._dw.clear()
                self._dr.clear()
            self._dw.extend(ws)
            self._dr.extend(rs)
            return f(*args, **kw)

        return wrapped


def build_nc():
    nc = bass.Bass("TRN2", target_bir_lowering=False)

    x_own = nc.dram_tensor("x_own", [TOK, D], F32, kind="ExternalInput")
    wqT = nc.dram_tensor("wqT", [D, D], F32, kind="ExternalInput")
    wkT = nc.dram_tensor("wkT", [D, D], F32, kind="ExternalInput")
    wvT = nc.dram_tensor("wvT", [D, D], F32, kind="ExternalInput")
    woT = nc.dram_tensor("woT", [D, D], F32, kind="ExternalInput")
    b_q = nc.dram_tensor("b_q", [P, DT], F32, kind="ExternalInput")
    b_k = nc.dram_tensor("b_k", [P, DT], F32, kind="ExternalInput")
    b_v = nc.dram_tensor("b_v", [P, DT], F32, kind="ExternalInput")
    b_o = nc.dram_tensor("b_o", [P, DT], F32, kind="ExternalInput")
    wr = nc.dram_tensor("wr", [D, E], F32, kind="ExternalInput")
    wg = nc.dram_tensor("wg", [E, D, D], F32, kind="ExternalInput")
    wu = nc.dram_tensor("wu", [E, D, D], F32, kind="ExternalInput")
    wdn = nc.dram_tensor("wdn", [E, D, D], F32, kind="ExternalInput")
    out = nc.dram_tensor("out", [TOK, D], F32, kind="ExternalOutput")

    q_loc = nc.dram_tensor("q_loc", [D, TOK], BF16, kind="Internal")
    k_loc = nc.dram_tensor("k_loc", [TOK, D], BF16, kind="Internal")
    v_loc = nc.dram_tensor("v_loc", [TOK, D], BF16, kind="Internal")
    k_full = nc.dram_tensor("k_full", [S, D], BF16, kind="Internal")
    v_full = nc.dram_tensor("v_full", [S, D], BF16, kind="Internal")

    dbg = {}
    for name, shape, dt in (
        ("dbg_q", [P, H, TOK], BF16),
        ("dbg_kv", [TOK, 2 * D], BF16),
        ("dbg_lnt", [P, DT, TOK], BF16),
        ("dbg_ctx", [P, H, TOK], BF16),
        ("dbg_x1", [P, NT, D], F32),
        ("dbg_w8", [P, E, NT], F32),
        ("dbg_ln2", [P, DT, TOK], BF16),
    ):
        if name in _DBG:
            dbg[name] = nc.dram_tensor(name, shape, dt, kind="ExternalOutput")

    es = ExitStack()
    X_res = es.enter_context(nc.sbuf_tensor("X_res", [P, NT, D], F32))
    actT = es.enter_context(nc.sbuf_tensor("actT", [P, DT, TOK], BF16))
    scr = es.enter_context(nc.sbuf_tensor("scr", [P, 16384], BF16))
    bq_sb = es.enter_context(nc.sbuf_tensor("bq_sb", [P, DT], F32))
    bk_sb = es.enter_context(nc.sbuf_tensor("bk_sb", [P, DT], F32))
    bv_sb = es.enter_context(nc.sbuf_tensor("bv_sb", [P, DT], F32))
    bo_sb = es.enter_context(nc.sbuf_tensor("bo_sb", [P, DT], F32))
    ident = es.enter_context(nc.sbuf_tensor("ident", [P, P], BF16))
    c_sum = es.enter_context(nc.sbuf_tensor("c_sum", [P, NT], F32))
    c_ssq = es.enter_context(nc.sbuf_tensor("c_ssq", [P, NT], F32))
    c_negm = es.enter_context(nc.sbuf_tensor("c_negm", [P, NT], F32))
    c_rstd = es.enter_context(nc.sbuf_tensor("c_rstd", [P, NT], F32))
    c_t0 = es.enter_context(nc.sbuf_tensor("c_t0", [P, NT], F32))
    c_t1 = es.enter_context(nc.sbuf_tensor("c_t1", [P, NT], F32))
    wr_sb = es.enter_context(nc.sbuf_tensor("wr_sb", [P, DT, E], F32))
    wr_hi = es.enter_context(nc.sbuf_tensor("wr_hi", [P, DT, E], BF16))
    wr_lo = es.enter_context(nc.sbuf_tensor("wr_lo", [P, DT, E], BF16))
    W_sel = es.enter_context(nc.sbuf_tensor("W_sel", [P, E, NT], F32))

    junk = scr[:, 0:D]
    xst = [scr[:, D + i * D: D + (i + 1) * D] for i in range(2)]
    kvst = [scr[:, 3 * D + i * 512: 3 * D + (i + 1) * 512] for i in range(4)]
    PT4 = scr[:, :].rearrange("p (a b) -> p a b", a=NKT)
    hlT = scr[:, :].rearrange("p (a b) -> p a b", a=DT)

    def sem(name):
        return es.enter_context(nc.semaphore(name))

    def emit_G():
        gO = sem("gO")
        with nc.Block() as blk:
            @blk.sync
            def _(sy):
                sy.dma_start(out=_ap(out, 0, [[D, P], [P * D, NT], [1, D]]),
                             in_=X_res[:, :, :]).then_inc(gO, 16)
                sy.wait_ge(gO, 16)

    def strip_src(mat, oc, expert=None):
        base = oc * 512
        if expert is not None:
            base += expert * D * D
        return _ap(mat, base, [[D, P], [P * D, DT], [1, 512]])

    aId = sem("aId")
    aX = sem("aX")
    aB = sem("aB")
    aWs = [sem("aW0"), sem("aW1")]
    aS1 = sem("aS1")
    aV2 = sem("aV2")
    aS2 = sem("aS2")
    aV3 = sem("aV3")
    aTt = sem("aTt")
    aEv = sem("aEv")
    aPm = sem("aPm")
    aPe = sem("aPe")
    aKVs = [sem(f"aKV{i}") for i in range(4)]
    ccK = sem("ccK")
    ccV = sem("ccV")


    a_strips = ([strip_src(wkT, oc) for oc in range(4)]
                + [strip_src(wvT, oc) for oc in range(4)]
                + [strip_src(wqT, oc) for oc in range(4)])
    NG = 96

    def a_group(idx):
        if idx < 32:
            oc, tt = divmod(idx, 8)
            return ("k", oc, oc, tt, None)
        if idx < 64:
            oc, tt = divmod(idx - 32, 8)
            return ("v", 4 + oc, oc, tt, None)
        qi = idx - 64
        oc, r = divmod(qi, 8)
        os_, g2 = divmod(r, 2)
        return ("q", 8 + oc, oc, os_, g2)

    with (
        nc.sbuf_tensor("wbufA0", [P, DT, 512], BF16) as wbA0,
        nc.sbuf_tensor("wbufA1", [P, DT, 512], BF16) as wbA1,
        nc.psum_tensor("tpA", [P, 8, P], BF16) as tpA,
        nc.psum_tensor("tpB", [P, 8, P], BF16) as tpB,
        nc.psum_tensor("tpC", [P, 8, P], BF16) as tpC,
        nc.psum_tensor("tpD", [P, 8, P], BF16) as tpD,
        nc.psum_tensor("pm0", [P, 512], F32) as pm0,
        nc.psum_tensor("pm1", [P, 512], F32) as pm1,
        nc.psum_tensor("pm2", [P, 512], F32) as pm2,
        nc.psum_tensor("pm3", [P, 512], F32) as pm3,
        nc.Block() as blk,
    ):
        wbuf = [wbA0, wbA1]
        tp = [[tpA, tpB], [tpC, tpD]]
        pmm = [pm0, pm1, pm2, pm3]

        @blk.sync
        def _(sy):
            sy.dma_start(out=X_res[:, :, :],
                         in_=_ap(x_own, 0, [[D, P], [P * D, NT], [1, D]])
                         ).then_inc(aX, 16)
            for bsrc, bdst in ((b_q, bq_sb), (b_k, bk_sb), (b_v, bv_sb),
                               (b_o, bo_sb)):
                sy.dma_start(out=bdst[:, :], in_=bsrc[:, :]).then_inc(aB, 16)
            sy.dma_start(out=wr_sb[:, :, :],
                         in_=_ap(wr, 0, [[E, P], [E * P, DT], [1, E]])
                         ).then_inc(aB, 16)
            for idx in range(96):
                kind, _, oc, a3, g2 = a_group(idx)
                sy.wait_ge(aPe, idx + 1)
                if kind in ("k", "v"):
                    dst = k_loc if kind == "k" else v_loc
                    out_ap = dst[a3 * P:(a3 + 1) * P,
                                 oc * 512:(oc + 1) * 512]
                else:
                    ot = oc * 4 + a3
                    out_ap = _ap(q_loc, ot * P * TOK + g2 * 512,
                                 [[TOK, P], [1, 512]])
                sy.dma_start(out=out_ap,
                             in_=kvst[idx % 4]).then_inc(aKVs[idx % 4], 16)
            sy.wait_ge(aB, 80)

        @blk.gpsimd
        def _(g):
            g = _SER(g)
            g.memset(ident[:, :], 0.0).then_inc(aId, 1)
            g.wait_ge(aId, 1)
            g.affine_select(out=ident[:, :], in_=ident[:, :],
                            compare_op=ALU.not_equal, fill=1.0, base=0,
                            pattern=[[-1, P]], channel_multiplier=1
                            ).then_inc(aId, 1)
            for i, src in enumerate(a_strips):
                if i == 8:
                    for q in range(4):
                        g.wait_ge(aKVs[q], 16 * 8)
                    g.collective_compute(
                        "AllGather", ALU.bypass,
                        replica_groups=[[0, 1, 2, 3], [4, 5, 6, 7]],
                        ins=[k_loc.ap().opt()], outs=[k_full.ap().opt()],
                    ).then_inc(ccK)
                if i >= 2:
                    g.wait_ge(aPe, (i - 2) * 8 + 8)
                g.dma_start(out=wbuf[i % 2][:, :, :], in_=src
                            ).then_inc(aWs[i % 2], 16)
            for q in range(4):
                g.wait_ge(aKVs[q], 16 * 16)
            g.collective_compute(
                "AllGather", ALU.bypass,
                replica_groups=[[0, 1, 2, 3], [4, 5, 6, 7]],
                ins=[v_loc.ap().opt()], outs=[v_full.ap().opt()],
            ).then_inc(ccV)

        @blk.scalar
        def _(s):
            s = _SER(s)
            s.wait_ge(aX, 16)
            for j in range(NT):
                s.activation(out=junk, in_=X_res[:, j, :], func=AF.Square,
                             accum_out=c_ssq[:, j:j + 1]).then_inc(aS1, 1)
            def a_evac(j):
                for half in range(2):
                    s.wait_ge(aTt, 2 * j + half + 1)
                    s.copy(
                        actT[:, half * 8:(half + 1) * 8, j * P:(j + 1) * P],
                        tp[j % 2][half][:, :, :]).then_inc(aEv, 1)
            for j in range(NT):
                s.wait_ge(aV2, j + 1)
                s.activation(out=c_rstd[:, j:j + 1], in_=c_t1[:, j:j + 1],
                             func=AF.Sqrt).then_inc(aS2, 1)
                if j >= 2:
                    a_evac(j - 2)
            a_evac(NT - 2)
            a_evac(NT - 1)

        @blk.vector
        def _(v):
            v = _SER(v)
            v.wait_ge(aX, 16)
            for j in range(NT):
                v.tensor_reduce(c_sum[:, j:j + 1], X_res[:, j, :], AX.X,
                                ALU.add)
                v.wait_ge(aS1, j + 1)
                v.tensor_scalar(c_t0[:, j:j + 1], c_sum[:, j:j + 1],
                                1.0 / D, None, ALU.mult)
                v.tensor_scalar(c_negm[:, j:j + 1], c_t0[:, j:j + 1],
                                -1.0, None, ALU.mult)
                v.tensor_scalar(c_t1[:, j:j + 1], c_ssq[:, j:j + 1],
                                1.0 / D, None, ALU.mult)
                v.scalar_tensor_tensor(c_t1[:, j:j + 1], c_t0[:, j:j + 1],
                                       c_negm[:, j:j + 1], c_t1[:, j:j + 1],
                                       ALU.mult, ALU.add)
                v.tensor_scalar(c_t1[:, j:j + 1], c_t1[:, j:j + 1],
                                EPS, None, ALU.add)
                v.reciprocal(c_t1[:, j:j + 1], c_t1[:, j:j + 1]
                             ).then_inc(aV2, 1)
                v.wait_ge(aS2, j + 1)
                if j >= 2:
                    v.wait_ge(aTt, 2 * (j - 2) + 2)
                v.tensor_scalar(xst[j % 2], X_res[:, j, :],
                                c_negm[:, j:j + 1], c_rstd[:, j:j + 1],
                                ALU.add, ALU.mult).then_inc(aV3, 1)
            for idx in range(NG):
                kind, _, oc, a3, g2 = a_group(idx)
                v.wait_ge(aPm, idx + 1)
                b = idx % 4
                if kind in ("k", "v"):
                    if idx >= 4:
                        v.wait_ge(aKVs[idx % 4], 16 * (idx // 4))
                    v.tensor_copy(kvst[idx % 4], pmm[b][:, :]).then_inc(aPe, 1)
                else:
                    ot = oc * 4 + a3
                    if idx == 64:
                        v.wait_ge(aB, 80)
                    if idx >= 4:
                        v.wait_ge(aKVs[idx % 4], 16 * (idx // 4))
                    v.tensor_scalar(kvst[idx % 4],
                                    pmm[b][:, :], bq_sb[:, ot:ot + 1],
                                    INV_C, ALU.add, ALU.mult).then_inc(aPe, 1)

        @blk.tensor
        def _(t):
            t.wait_ge(aId, 2)
            for j in range(NT):
                t.wait_ge(aV3, j + 1)
                if j >= 2:
                    t.wait_ge(aEv, 2 * (j - 2) + 2)
                for dt in range(DT):
                    bank = tp[j % 2][dt // 8]
                    ins = t.transpose(bank[:, dt % 8, :],
                                      xst[j % 2][:, dt * P:(dt + 1) * P],
                                      ident[:, :])
                    if dt % 8 == 7:
                        ins.then_inc(aTt, 1)
            for idx in range(NG):
                kind, strip, oc, a3, g2 = a_group(idx)
                t.wait_ge(aWs[strip % 2], 16 * (strip // 2 + 1))
                if kind in ("k", "v"):
                    t.wait_ge(aEv, 2 * a3 + 2)
                else:
                    t.wait_ge(aEv, 2 * NT)
                if idx >= 4:
                    t.wait_ge(aPe, idx - 3)
                b = idx % 4
                for dt in range(DT):
                    if kind in ("k", "v"):
                        ins = t.matmul(pmm[b][:, :],
                                       actT[:, dt, a3 * P:(a3 + 1) * P],
                                       wbuf[strip % 2][:, dt, :],
                                       start=(dt == 0), stop=(dt == DT - 1))
                    else:
                        ins = t.matmul(
                            pmm[b][:, :],
                            wbuf[strip % 2][:, dt, a3 * P:(a3 + 1) * P],
                            actT[:, dt, g2 * 512:(g2 + 1) * 512],
                            start=(dt == 0), stop=(dt == DT - 1))
                ins.then_inc(aPm, 1)

    if {"dbg_kv", "dbg_q", "dbg_lnt"} & set(dbg):
        dS = sem("dS")
        with nc.Block() as blk:
            @blk.sync
            def _(sy):
                n = 0
                for key, src in (("dbg_kv", None),
                                 ("dbg_lnt", actT[:, :, :])):
                    if key == "dbg_kv" and key in dbg:
                        sy.dma_start(out=dbg[key].ap()[:, 0:D], in_=k_loc.ap()
                                     ).then_inc(dS, 16)
                        sy.dma_start(out=dbg[key].ap()[:, D:2 * D],
                                     in_=v_loc.ap()).then_inc(dS, 16)
                        n += 32
                        continue
                    if key in dbg:
                        sy.dma_start(out=dbg[key].ap(), in_=src
                                     ).then_inc(dS, 16)
                        n += 16
                sy.wait_ge(dS, n)

    if _STOP <= "B":
        emit_G()
        es.close()
        return nc, dbg
    cKHs = [sem("cKH0"), sem("cKH1")]
    cKb = sem("cKb")
    cVHs = [sem("cVH0"), sem("cVH1")]
    cSm = sem("cSm")
    cMx = sem("cMx")
    cMl = sem("cMl")
    cEx = sem("cEx")
    cFa = sem("cFa")
    cNm = sem("cNm")
    cPTs = [sem(f"cPT{i}") for i in range(4)]
    cPV = sem("cPV")
    cCx = sem("cCx")

    cQHs = [sem("cQH0"), sem("cQH1")]
    with (
        nc.sbuf_tensor("Q_h", [P, 2, TOK], BF16) as Q_h,
        nc.sbuf_tensor("K_h", [P, 2, S], BF16) as K_h,
        nc.sbuf_tensor("V_h", [P, 2, NKT, P], BF16) as V_h,
        nc.sbuf_tensor("P_buf", [P, 2, S], BF16) as P_buf,
        nc.sbuf_tensor("negm8", [P, 2, 8], F32) as negm8,
        nc.sbuf_tensor("sum8", [P, 2, 8], F32) as sum8,
        nc.sbuf_tensor("fac8", [P, 2, 8], F32) as fac8,
        nc.sbuf_tensor("c_mall", [P, 2], F32) as c_mall,
        nc.sbuf_tensor("c_sall", [P, 2], F32) as c_sall,
        nc.sbuf_tensor("junk8", [P, 8], F32) as junk8,
        nc.psum_tensor("ps_s", [P, 4, 512], F32) as ps_s,
        nc.psum_tensor("ps_c0", [P, 512], F32) as ps_c0,
        nc.psum_tensor("ps_c1", [P, 512], F32) as ps_c1,
        nc.Block() as blk,
    ):
        ps_c = [ps_c0, ps_c1]

        @blk.sync
        def _(sy):
            def k_load(h):
                if h >= H:
                    return
                if h == 0:
                    sy.wait_ge(ccK, 1)
                if h >= 2:
                    sy.wait_ge(cSm, 64 * (h - 1))
                sy.dma_start_transpose(
                    out=K_h[:, h % 2, :],
                    in_=k_full[:, h * P:(h + 1) * P]).then_inc(cKHs[h % 2], 16)

            for q in range(4):
                sy.wait_ge(aKVs[q], 16 * 24)
            k_load(0)
            k_load(1)
            for gq in range(H * NT):
                sl, gb = gq % 4, gq // 4
                sy.wait_ge(cNm, gq * 8 + 8)
                sy.wait_ge(cPV, gb)
                sy.dma_start_transpose(
                    out=PT4[:, :, sl * P:(sl + 1) * P],
                    in_=P_buf[:, gq % 2, :].rearrange(
                        "q (kt p) -> q kt p", kt=NKT)
                    ).then_inc(cPTs[sl], 16)
                if gq % 8 == 5:
                    k_load(gq // 8 + 2)

        @blk.scalar
        def _(s):
            s = _SER(s)
            for q in range(4):
                s.wait_ge(aKVs[q], 16 * 24)
            for h in range(H):
                if h >= 2:
                    s.wait_ge(cSm, 64 * (h - 1))
                s.dma_start(out=Q_h[:, h % 2, :],
                            in_=q_loc[h * P:(h + 1) * P, :]
                            ).then_inc(cQHs[h % 2], 16)
                if h == 0:
                    s.wait_ge(ccV, 1)
                if h >= 2:
                    s.wait_ge(cPV, 2 * h - 2)
                s.dma_start(
                    out=V_h[:, h % 2, :, :],
                    in_=_ap(v_full, h * P, [[D, P], [P * D, NKT], [1, P]])
                    ).then_inc(cVHs[h % 2], 16)
                for jq in range(NT):
                    gq = h * NT + jq
                    jr = jq % 2
                    if gq >= 2:
                        psl = (gq - 2) % 4
                        s.wait_ge(cPTs[psl], 16 * ((gq - 2) // 4 + 1))
                    if gq >= 2:
                        s.wait_ge(cFa, gq - 1)
                    for kc in range(8):
                        gc = gq * 8 + kc
                        if kc % 2 == 0:
                            s.wait_ge(cMx, gc // 2 + 1)
                        s.activation(out=P_buf[:, gq % 2,
                                                kc * 512:(kc + 1) * 512],
                                     in_=ps_s[:, kc % 4, :], func=AF.Exp,
                                     bias=negm8[:, jr, kc:kc + 1], scale=1.0,
                                     accum_out=sum8[:, jr, kc:kc + 1]
                                     ).then_inc(cEx, 1)
                    s.wait_ge(cMl, gq + 1)
                    s.activation(out=fac8[:, jr, :], in_=negm8[:, jr, :],
                                 func=AF.Exp, bias=c_mall[:, jr:jr + 1],
                                 scale=-1.0).then_inc(cEx, 1)

        @blk.vector
        def _(v):
            v = _SER(v)

            def v_ctx_evac(kb):
                v.wait_ge(cPV, kb + 1)
                v.tensor_scalar(actT[:, kb // 2,
                                     (kb % 2) * 512:(kb % 2) * 512 + 512],
                                ps_c[kb % 2][:, :],
                                bv_sb[:, kb // 2:kb // 2 + 1],
                                None, ALU.add).then_inc(cCx, 1)

            def v_stats_norm(gq):
                jr = (gq % NT) % 2
                v.wait_ge(cEx, gq * 9 + 9)
                v.tensor_mul(junk8[:, :], sum8[:, jr, :], fac8[:, jr, :])
                v.tensor_reduce(c_sall[:, jr:jr + 1], junk8[:, :],
                                AX.X, ALU.add)
                v.reciprocal(c_sall[:, jr:jr + 1], c_sall[:, jr:jr + 1])
                v.tensor_scalar(fac8[:, jr, :], fac8[:, jr, :],
                                c_sall[:, jr:jr + 1], None, ALU.mult
                                ).then_inc(cFa, 1)
                for kc in range(8):
                    v.tensor_scalar(P_buf[:, gq % 2,
                                          kc * 512:(kc + 1) * 512],
                                    P_buf[:, gq % 2,
                                          kc * 512:(kc + 1) * 512],
                                    fac8[:, jr, kc:kc + 1], None,
                                    ALU.mult).then_inc(cNm, 1)

            prev = None
            for h in range(H):
                v.wait_ge(cKHs[h % 2], 16 * (h // 2 + 1))
                if h == 0:
                    v.wait_ge(aB, 80)
                v.tensor_scalar(K_h[:, h % 2, :], K_h[:, h % 2, :],
                                bk_sb[:, h:h + 1], None, ALU.add
                                ).then_inc(cKb, 1)
                for jq in range(NT):
                    gq = h * NT + jq
                    jr = jq % 2
                    for kp in range(4):
                        v.wait_ge(cSm, gq * 8 + 2 * kp + 2)
                        v.tensor_reduce(negm8[:, jr, 2 * kp:2 * kp + 2],
                                        ps_s[:, 2 * (kp % 2):2 * (kp % 2) + 2, :],
                                        AX.X, ALU.max,
                                        negate=True).then_inc(cMx, 1)
                    v.tensor_reduce(c_mall[:, jr:jr + 1], negm8[:, jr, :],
                                    AX.X, ALU.min).then_inc(cMl, 1)
                    if prev is not None:
                        v_stats_norm(prev)
                    prev = gq
                    if gq >= 5 and (gq - 5) % 4 == 0:
                        v_ctx_evac((gq - 5) // 4)
            v_stats_norm(prev)
            v_ctx_evac(30)
            v_ctx_evac(31)

        @blk.tensor
        def _(t):
            def t_pv(gb):
                hh = gb // 2
                for q in range(4):
                    t.wait_ge(cPTs[q], 16 * (gb + 1))
                if gb >= 2:
                    t.wait_ge(cCx, gb - 1)
                for kt in range(NKT):
                    ins = t.matmul(ps_c[gb % 2][:, :],
                                   V_h[:, hh % 2, kt, :],
                                   PT4[:, kt, :],
                                   start=(kt == 0),
                                   stop=(kt == NKT - 1))
                ins.then_inc(cPV, 1)

            for h in range(H):
                t.wait_ge(cKb, h + 1)
                t.wait_ge(cQHs[h % 2], 16 * (h // 2 + 1))
                t.wait_ge(cVHs[h % 2], 16 * (h // 2 + 1))
                for jq in range(NT):
                    gq = h * NT + jq
                    for kc in range(8):
                        n = gq * 8 + kc
                        if n >= 4 and kc % 2 == 0:
                            pg, pk = divmod(n - 3, 8)
                            t.wait_ge(cEx, pg * 9 + pk + 1)
                        t.matmul(ps_s[:, kc % 4, :],
                                 Q_h[:, h % 2, jq * P:(jq + 1) * P],
                                 K_h[:, h % 2, kc * 512:(kc + 1) * 512],
                                 start=True, stop=True).then_inc(cSm, 1)
                    if gq % 4 == 0 and gq >= 4:
                        t_pv(gq // 4 - 1)
            t_pv(31)


    if "dbg_ctx" in dbg:
        dC = sem("dC")
        with nc.Block() as blk:
            @blk.sync
            def _(sy):
                sy.dma_start(out=dbg["dbg_ctx"].ap(), in_=actT[:, :, :]
                             ).then_inc(dC, 16)
                sy.wait_ge(dC, 16)

    if _STOP <= "C":
        emit_G()
        es.close()
        return nc, dbg

    dWs = [sem("dW0"), sem("dW1")]
    dMM = sem("dMM")
    dEv = sem("dEv")
    dTp = sem("dTp")
    dAdd = sem("dAdd")

    with (
        nc.sbuf_tensor("wbufD0", [P, DT, 512], BF16) as wbD0,
        nc.sbuf_tensor("wbufD1", [P, DT, 512], BF16) as wbD1,
        nc.sbuf_tensor("stageO", [P, 2, 512], BF16) as stageO,
        nc.psum_tensor("pd0", [P, 512], F32) as pd0,
        nc.psum_tensor("pd1", [P, 512], F32) as pd1,
        nc.psum_tensor("ptd0", [P, 4, P], BF16) as ptd0,
        nc.psum_tensor("ptd1", [P, 4, P], BF16) as ptd1,
        nc.Block() as blk,
    ):
        wbufD = [wbD0, wbD1]
        pd = [pd0, pd1]
        ptd = [ptd0, ptd1]

        def d_tp(t, pidx):
            t.wait_ge(dEv, pidx + 1)
            if pidx >= 2:
                t.wait_ge(dAdd, pidx - 1)
            for k in range(4):
                ins = t.transpose(ptd[pidx % 2][:, k, :],
                                  stageO[:, pidx % 2, k * P:(k + 1) * P],
                                  ident[:, :])
            ins.then_inc(dTp, 1)

        def d_add(v, pidx):
            poc, pr = divmod(pidx, 8)
            pos, pg2 = divmod(pr, 2)
            pot = poc * 4 + pos
            v.wait_ge(dTp, pidx + 1)
            v.tensor_tensor(
                X_res[:, pg2 * 4:pg2 * 4 + 4, pot * P:(pot + 1) * P],
                ptd[pidx % 2][:, :, :],
                X_res[:, pg2 * 4:pg2 * 4 + 4, pot * P:(pot + 1) * P],
                ALU.add).then_inc(dAdd, 1)

        @blk.gpsimd
        def _(g):
            g = _SER(g)
            for oc in range(4):
                if oc >= 2:
                    g.wait_ge(dEv, (oc - 2) * 8 + 8)
                g.dma_start(out=wbufD[oc % 2][:, :, :],
                            in_=strip_src(woT, oc)).then_inc(dWs[oc % 2], 16)

        @blk.tensor
        def _(t):
            for idx in range(32):
                oc, r = divmod(idx, 8)
                os_, g2 = divmod(r, 2)
                t.wait_ge(dWs[oc % 2], 16 * (oc // 2 + 1))
                if idx >= 2:
                    t.wait_ge(dEv, idx - 1)
                for ct in range(DT):
                    ins = t.matmul(pd[idx % 2][:, :],
                                   wbufD[oc % 2][:, ct, os_ * P:(os_ + 1) * P],
                                   actT[:, ct, g2 * 512:(g2 + 1) * 512],
                                   start=(ct == 0), stop=(ct == DT - 1))
                ins.then_inc(dMM, 1)
                if idx >= 1:
                    d_tp(t, idx - 1)
            d_tp(t, 31)

        @blk.vector
        def _(v):
            v = _SER(v)
            for idx in range(32):
                oc, r = divmod(idx, 8)
                os_, g2 = divmod(r, 2)
                ot = oc * 4 + os_
                v.wait_ge(dMM, idx + 1)
                if idx >= 2:
                    v.wait_ge(dTp, idx - 1)
                v.tensor_scalar(stageO[:, idx % 2, :], pd[idx % 2][:, :],
                                bo_sb[:, ot:ot + 1], None, ALU.add
                                ).then_inc(dEv, 1)
                if idx >= 1:
                    d_add(v, idx - 1)
            d_add(v, 31)

    if "dbg_x1" in dbg:
        dX = sem("dX")
        with nc.Block() as blk:
            @blk.sync
            def _(sy):
                sy.dma_start(out=dbg["dbg_x1"].ap(), in_=X_res[:, :, :]
                             ).then_inc(dX, 16)
                sy.wait_ge(dX, 16)

    if _STOP <= "D":
        emit_G()
        es.close()
        return nc, dbg

    eS1 = sem("eS1")
    eV2 = sem("eV2")
    eS2 = sem("eS2")
    eV3 = sem("eV3")
    eTt = sem("eTt")
    eEv = sem("eEv")
    eRm = sem("eRm")
    eRe = sem("eRe")
    eRx = sem("eRx")
    eRw = sem("eRw")

    with (
        nc.sbuf_tensor("stgf", [P, 2, D], F32) as stgf,
        nc.sbuf_tensor("stghi", [P, 2, D], BF16) as stghi,
        nc.sbuf_tensor("stglo", [P, 2, D], BF16) as stglo,
        nc.sbuf_tensor("junkE", [P, D], BF16) as junkE,
        nc.sbuf_tensor("lg_sb", [P, NT, E], F32) as lg_sb,
        nc.sbuf_tensor("e8", [P, E], F32) as e8,
        nc.sbuf_tensor("r_m1", [P, 1], F32) as r_m1,
        nc.sbuf_tensor("r_nm1", [P, 1], F32) as r_nm1,
        nc.sbuf_tensor("r_m2", [P, 1], F32) as r_m2,
        nc.sbuf_tensor("r_msk", [P, E], F32) as r_msk,
        nc.sbuf_tensor("r_sel", [P, E], F32) as r_sel,
        nc.sbuf_tensor("r_wun", [P, E], F32) as r_wun,
        nc.sbuf_tensor("r_ss", [P, 1], F32) as r_ss,
        nc.psum_tensor("tpE0", [P, 8, P], BF16) as tpE0,
        nc.psum_tensor("tpE1", [P, 8, P], BF16) as tpE1,
        nc.psum_tensor("tpE2", [P, 8, P], BF16) as tpE2,
        nc.psum_tensor("tpE3", [P, 8, P], BF16) as tpE3,
        nc.psum_tensor("prt", [P, E], F32) as prt,
        nc.Block() as blk,
    ):
        tpE = [tpE0, tpE1, tpE2, tpE3]

        @blk.scalar
        def _(s):
            s = _SER(s)
            for j in range(NT):
                s.activation(out=junkE[:, :], in_=X_res[:, j, :], func=AF.Square,
                             accum_out=c_ssq[:, j:j + 1]).then_inc(eS1, 1)
            def e_evac(j):
                for q in range(4):
                    s.wait_ge(eTt, 4 * j + q + 1)
                    dst = actT if q < 2 else hlT
                    half = q % 2
                    s.copy(
                        dst[:, half * 8:(half + 1) * 8, j * P:(j + 1) * P],
                        tpE[q][:, :, :]).then_inc(eEv, 1)
            for j in range(NT):
                s.wait_ge(eV2, j + 1)
                s.activation(out=c_rstd[:, j:j + 1], in_=c_t1[:, j:j + 1],
                             func=AF.Sqrt).then_inc(eS2, 1)
                if j >= 2:
                    e_evac(j - 2)
            e_evac(NT - 2)
            e_evac(NT - 1)
            for j in range(NT):
                s.wait_ge(eRe, 3 * j + 3)
                s.activation(out=e8[:, :], in_=lg_sb[:, j, :], func=AF.Exp,
                             bias=r_nm1[:, :], scale=1.0).then_inc(eRx, 1)

        @blk.vector
        def _(v):
            v = _SER(v)
            for j in range(NT):
                v.tensor_reduce(c_sum[:, j:j + 1], X_res[:, j, :], AX.X,
                                ALU.add)
                v.wait_ge(eS1, j + 1)
                v.tensor_scalar(c_t0[:, j:j + 1], c_sum[:, j:j + 1],
                                1.0 / D, None, ALU.mult)
                v.tensor_scalar(c_negm[:, j:j + 1], c_t0[:, j:j + 1],
                                -1.0, None, ALU.mult)
                v.tensor_scalar(c_t1[:, j:j + 1], c_ssq[:, j:j + 1],
                                1.0 / D, None, ALU.mult)
                v.scalar_tensor_tensor(c_t1[:, j:j + 1], c_t0[:, j:j + 1],
                                       c_negm[:, j:j + 1], c_t1[:, j:j + 1],
                                       ALU.mult, ALU.add)
                v.tensor_scalar(c_t1[:, j:j + 1], c_t1[:, j:j + 1],
                                EPS, None, ALU.add)
                v.reciprocal(c_t1[:, j:j + 1], c_t1[:, j:j + 1]
                             ).then_inc(eV2, 1)
                v.wait_ge(eS2, j + 1)
                if j >= 2:
                    v.wait_ge(eTt, 4 * (j - 2) + 4)
                v.tensor_scalar(stgf[:, j % 2, :], X_res[:, j, :],
                                c_negm[:, j:j + 1], c_rstd[:, j:j + 1],
                                ALU.add, ALU.mult)
                v.tensor_copy(stghi[:, j % 2, :], stgf[:, j % 2, :]
                              ).then_inc(eV3, 1)
                v.tensor_sub(stglo[:, j % 2, :], stgf[:, j % 2, :],
                             stghi[:, j % 2, :]).then_inc(eV3, 1)
            v.tensor_copy(wr_hi[:, :, :], wr_sb[:, :, :])
            v.tensor_sub(wr_lo[:, :, :], wr_sb[:, :, :], wr_hi[:, :, :]
                         ).then_inc(eV3, 1)
            for j in range(NT):
                v.wait_ge(eRm, j + 1)
                v.tensor_copy(lg_sb[:, j, :], prt[:, :]).then_inc(eRe, 1)
                v.tensor_reduce(r_m1[:, :], lg_sb[:, j, :], AX.X, ALU.max)
                if j >= 1:
                    v.wait_ge(eRx, j)
                v.tensor_scalar(r_nm1[:, :], r_m1[:, :], -1.0, None,
                                ALU.mult).then_inc(eRe, 1)
                v.tensor_scalar(r_msk[:, :], lg_sb[:, j, :], r_m1[:, :],
                                -1e30, ALU.is_ge, ALU.mult)
                v.tensor_add(r_msk[:, :], lg_sb[:, j, :], r_msk[:, :])
                v.tensor_reduce(r_m2[:, :], r_msk[:, :], AX.X, ALU.max
                                ).then_inc(eRe, 1)
                v.wait_ge(eRx, j + 1)
                v.tensor_scalar(r_sel[:, :], lg_sb[:, j, :], r_m2[:, :],
                                None, ALU.is_ge)
                v.tensor_mul(r_wun[:, :], e8[:, :], r_sel[:, :])
                v.tensor_reduce(r_ss[:, :], r_wun[:, :], AX.X, ALU.add)
                v.reciprocal(r_ss[:, :], r_ss[:, :])
                v.tensor_scalar(W_sel[:, :, j], r_wun[:, :], r_ss[:, :],
                                None, ALU.mult).then_inc(eRw, 1)

        @blk.tensor
        def _(t):
            for j in range(NT):
                for pas in range(2):
                    t.wait_ge(eV3, 2 * j + pas + 1)
                    if j >= 1:
                        t.wait_ge(eEv, 4 * (j - 1) + 2 * pas + 2)
                    src = stghi if pas == 0 else stglo
                    for dt in range(DT):
                        bank = tpE[2 * pas + dt // 8]
                        ins = t.transpose(bank[:, dt % 8, :],
                                          src[:, j % 2, dt * P:(dt + 1) * P],
                                          ident[:, :])
                        if dt % 8 == 7:
                            ins.then_inc(eTt, 1)
            t.wait_ge(eV3, 2 * NT + 1)
            t.wait_ge(eEv, 4 * NT)
            for j in range(NT):
                if j >= 1:
                    t.wait_ge(eRe, 3 * (j - 1) + 1)
                nmm = 0
                for lhs_src, rhs_src in ((actT, wr_hi), (actT, wr_lo),
                                         (hlT, wr_hi)):
                    for dt in range(DT):
                        ins = t.matmul(prt[:, :],
                                       lhs_src[:, dt, j * P:(j + 1) * P],
                                       rhs_src[:, dt, :],
                                       start=(nmm == 0), stop=(nmm == 47))
                        nmm += 1
                ins.then_inc(eRm, 1)

    if {"dbg_w8", "dbg_ln2"} & set(dbg):
        dW8 = sem("dW8")
        with nc.Block() as blk:
            @blk.sync
            def _(sy):
                n = 0
                for key, src in (("dbg_w8", W_sel[:, :, :]),
                                 ("dbg_ln2", actT[:, :, :])):
                    if key in dbg:
                        sy.dma_start(out=dbg[key].ap(), in_=src
                                     ).then_inc(dW8, 16)
                        n += 16
                sy.wait_ge(dW8, n)

    if _STOP <= "E":
        emit_G()
        es.close()
        return nc, dbg

    fWs = [sem("fW0"), sem("fW1")]
    fMM = sem("fMM")
    fEvG = sem("fEvG")
    fEvV = sem("fEvV")
    fTp = sem("fTp")
    fAdd = sem("fAdd")

    f_strips = []
    for i in range(E):
        for fc in range(4):
            f_strips.append(strip_src(wg, fc, i))
            f_strips.append(strip_src(wu, fc, i))
        for oc in range(4):
            f_strips.append(strip_src(wdn, oc, i))

    fgroups = []
    for i in range(E):
        for fc in range(4):
            for fs in range(4):
                for g2 in range(2):
                    fgroups.append(("g", i, fc, fs, g2, i * 12 + 2 * fc))
            for fs in range(4):
                for g2 in range(2):
                    fgroups.append(("u", i, fc, fs, g2, i * 12 + 2 * fc + 1))
        for oc in range(4):
            for os_ in range(4):
                for g2 in range(2):
                    fgroups.append(("d", i, oc, os_, g2, i * 12 + 8 + oc))
    NFG = len(fgroups)
    evac_of = []
    nG = nV = 0
    d_index = {}
    g_evac_idx = {}
    u_evac_idx = {}
    ndi = 0
    for m, (kind, i, c, s_, g2, si) in enumerate(fgroups):
        if kind == "g":
            nG += 1
            evac_of.append(("G", nG))
            g_evac_idx[(i, c, s_, g2)] = nG
        else:
            nV += 1
            evac_of.append(("V", nV))
            if kind == "u":
                u_evac_idx[(i, c, s_, g2)] = nV
            else:
                d_index[m] = ndi
                ndi += 1
    NDG = ndi
    d_list = [m for m in range(NFG) if fgroups[m][0] == "d"]
    strip_last = {}
    for m, grp in enumerate(fgroups):
        strip_last[grp[5]] = m

    with (
        nc.sbuf_tensor("wbufF0", [P, DT, 512], BF16) as wbF0,
        nc.sbuf_tensor("wbufF1", [P, DT, 512], BF16) as wbF1,
        nc.sbuf_tensor("sg_buf", [P, 4, TOK], BF16) as sg_buf,
        nc.sbuf_tensor("stageE", [P, 2, 512], BF16) as stageE,
        nc.psum_tensor("pf0", [P, 512], F32) as pf0,
        nc.psum_tensor("pf1", [P, 512], F32) as pf1,
        nc.psum_tensor("pf2", [P, 512], F32) as pf2,
        nc.psum_tensor("pf3", [P, 512], F32) as pf3,
        nc.psum_tensor("ptf0", [P, 4, P], BF16) as ptf0,
        nc.psum_tensor("ptf1", [P, 4, P], BF16) as ptf1,
        nc.Block() as blk,
    ):
        wbufF = [wbF0, wbF1]
        pf = [pf0, pf1, pf2, pf3]
        ptf = [ptf0, ptf1]

        def f_tp(t, di):
            m = d_list[di]
            _, ei = evac_of[m]
            t.wait_ge(fEvV, ei)
            if di >= 2:
                t.wait_ge(fAdd, 4 * (di - 2) + 4)
            for k in range(4):
                ins = t.transpose(ptf[di % 2][:, k, :],
                                  stageE[:, di % 2, k * P:(k + 1) * P],
                                  ident[:, :])
            ins.then_inc(fTp, 1)

        def f_add(v, di):
            m = d_list[di]
            _, i, oc, os_, g2, _ = fgroups[m]
            ot = oc * 4 + os_
            v.wait_ge(fTp, di + 1)
            for k in range(4):
                j = g2 * 4 + k
                v.scalar_tensor_tensor(
                    X_res[:, j, ot * P:(ot + 1) * P],
                    ptf[di % 2][:, k, :], W_sel[:, i, j:j + 1],
                    X_res[:, j, ot * P:(ot + 1) * P],
                    ALU.mult, ALU.add).then_inc(fAdd, 1)

        @blk.gpsimd
        def _(g):
            g = _SER(g)
            for si, src in enumerate(f_strips):
                if si >= 2:
                    ek, ei = evac_of[strip_last[si - 2]]
                    g.wait_ge(fEvG if ek == "G" else fEvV, ei)
                g.dma_start(out=wbufF[si % 2][:, :, :], in_=src
                            ).then_inc(fWs[si % 2], 16)

        @blk.tensor
        def _(t):
            for m, (kind, i, c, s_, g2, si) in enumerate(fgroups):
                t.wait_ge(fWs[si % 2], 16 * (si // 2 + 1))
                if m >= 4:
                    ek, ei = evac_of[m - 4]
                    t.wait_ge(fEvG if ek == "G" else fEvV, ei)
                if kind == "d" and c == 0 and s_ == 0 and g2 == 0:
                    t.wait_ge(fEvV, u_evac_idx[(i, 3, 3, 1)])
                b = m % 4
                rhs_base = actT if kind in ("g", "u") else hlT
                for dt in range(DT):
                    ins = t.matmul(pf[b][:, :],
                                   wbufF[si % 2][:, dt, s_ * P:(s_ + 1) * P],
                                   rhs_base[:, dt, g2 * 512:(g2 + 1) * 512],
                                   start=(dt == 0), stop=(dt == DT - 1))
                ins.then_inc(fMM, 1)
                if kind == "d":
                    di = d_index[m]
                    if di >= 1:
                        f_tp(t, di - 1)
            f_tp(t, NDG - 1)

        @blk.scalar
        def _(s):
            s = _SER(s)
            for m, (kind, i, c, s_, g2, si) in enumerate(fgroups):
                if kind != "g":
                    continue
                s.wait_ge(fMM, m + 1)
                if c >= 1:
                    s.wait_ge(fEvV, u_evac_idx[(i, c - 1, s_, g2)])
                elif i >= 1:
                    s.wait_ge(fEvV, u_evac_idx[(i - 1, 3, s_, g2)])
                s.activation(out=sg_buf[:, s_, g2 * 512:(g2 + 1) * 512],
                             in_=pf[m % 4][:, :], func=AF.Silu
                             ).then_inc(fEvG, 1)

        @blk.vector
        def _(v):
            v = _SER(v)
            for m, (kind, i, c, s_, g2, si) in enumerate(fgroups):
                if kind == "g":
                    continue
                v.wait_ge(fMM, m + 1)
                if kind == "u":
                    v.wait_ge(fEvG, g_evac_idx[(i, c, s_, g2)])
                    v.tensor_tensor(
                        hlT[:, c * 4 + s_, g2 * 512:(g2 + 1) * 512],
                        sg_buf[:, s_, g2 * 512:(g2 + 1) * 512],
                        pf[m % 4][:, :], ALU.mult).then_inc(fEvV, 1)
                else:
                    di = d_index[m]
                    if di >= 2:
                        v.wait_ge(fTp, di - 1)
                    v.tensor_copy(stageE[:, di % 2, :], pf[m % 4][:, :]
                                  ).then_inc(fEvV, 1)
                    if di >= 1:
                        f_add(v, di - 1)
            f_add(v, NDG - 1)

    emit_G()
    es.close()
    return nc, dbg


_cache = {}


def _get_nc():
    key = (tuple(sorted(_DBG)), _STOP)
    if key not in _cache:
        _cache[key] = build_nc()
    return _cache[key]


def kernel(L0_attn_x, L0_attn_wq, L0_attn_bq, L0_attn_wk, L0_attn_bk,
           L0_attn_wv, L0_attn_bv, L0_attn_wo, L0_attn_bo,
           L0_ffn_Wr, L0_ffn_Wg, L0_ffn_Wu, L0_ffn_Wd, _extra_out=None):
    x = np.asarray(L0_attn_x, dtype=np.float32)
    shared = {
        "wqT": np.ascontiguousarray(np.asarray(L0_attn_wq, np.float32).T),
        "wkT": np.ascontiguousarray(np.asarray(L0_attn_wk, np.float32).T),
        "wvT": np.ascontiguousarray(np.asarray(L0_attn_wv, np.float32).T),
        "woT": np.ascontiguousarray(np.asarray(L0_attn_wo, np.float32).T),
        "b_q": np.ascontiguousarray(
            np.asarray(L0_attn_bq, np.float32).reshape(DT, P).T),
        "b_k": np.ascontiguousarray(
            np.asarray(L0_attn_bk, np.float32).reshape(DT, P).T),
        "b_v": np.ascontiguousarray(
            np.asarray(L0_attn_bv, np.float32).reshape(DT, P).T),
        "b_o": np.ascontiguousarray(
            np.asarray(L0_attn_bo, np.float32).reshape(DT, P).T),
        "wr": np.ascontiguousarray(np.asarray(L0_ffn_Wr, np.float32)),
        "wg": np.ascontiguousarray(np.asarray(L0_ffn_Wg, np.float32)),
        "wu": np.ascontiguousarray(np.asarray(L0_ffn_Wu, np.float32)),
        "wdn": np.ascontiguousarray(np.asarray(L0_ffn_Wd, np.float32)),
    }
    nc, dbg = _get_nc()
    in_maps = []
    for c in range(8):
        b, qb = divmod(c, 4)
        m = dict(shared)
        m["x_own"] = np.ascontiguousarray(x[b, qb * TOK:(qb + 1) * TOK])
        in_maps.append(m)
    res = run_bass_kernel_spmd(nc, in_maps, core_ids=list(range(8)))
    outf = np.empty((2, S, D), np.float32)
    for c in range(8):
        b, qb = divmod(c, 4)
        outf[b, qb * TOK:(qb + 1) * TOK] = res.results[c]["out"]
    if _extra_out is not None:
        for c in range(8):
            _extra_out.append({k: np.asarray(res.results[c][k]) for k in dbg})
    return outf


# version 10
# speedup vs baseline: 1.0946x; 1.0614x over previous
import math
from contextlib import ExitStack

import numpy as np

import concourse.bass as bass
import concourse.mybir as mybir
from concourse.bass_utils import run_bass_kernel_spmd

F32 = mybir.dt.float32
BF16 = mybir.dt.bfloat16
AF = mybir.ActivationFunctionType
ALU = mybir.AluOpType
AX = mybir.AxisListType

P = 128
D = 2048
TOK = 1024
S = 4096
NT = TOK // P
DT = D // P
H = 16
NKT = S // P
E = 8
INV_C = 1.0 / math.sqrt(128.0)
EPS = 1e-5

_DBG = set()
_STOP = "Z"


def _ap(t, offset, dims):
    return bass.AP(t, offset, dims)


def _ap_range(a):
    lo = a.offset
    ext = 1
    dims = list(a.ap)
    for step, count in dims[1:]:
        ext += abs(step) * (count - 1)
    return (a.tensor.name, lo, lo + ext)


def _overlaps(x, y):
    return x[0] == y[0] and x[1] < y[2] and y[1] < x[2]


class _SER:

    _WR = {"out", "accum_out"}
    _PASS = {"wait_ge", "drain", "nop", "sem_inc", "dma_start",
             "dma_start_transpose", "collective_compute", "wait_op",
             "memzero"}

    def __init__(self, e):
        self._e = e
        self._dw = []
        self._dr = []

    def __getattr__(self, name):
        f = getattr(self._e, name)
        if name in self._PASS or not callable(f):
            return f

        def wrapped(*args, **kw):
            ws, rs = [], []
            for i, a in enumerate(args):
                if isinstance(a, bass.AP):
                    (ws if i == 0 else rs).append(_ap_range(a))
            for k, a in kw.items():
                if isinstance(a, bass.AP):
                    (ws if k in self._WR else rs).append(_ap_range(a))
            conflict = any(_overlaps(t, w) for t in ws + rs
                           for w in self._dw) or                        any(_overlaps(w, r) for w in ws for r in self._dr)
            if conflict:
                self._e.drain()
                self._dw.clear()
                self._dr.clear()
            self._dw.extend(ws)
            self._dr.extend(rs)
            return f(*args, **kw)

        return wrapped


def build_nc():
    nc = bass.Bass("TRN2", target_bir_lowering=False)

    x_own = nc.dram_tensor("x_own", [TOK, D], F32, kind="ExternalInput")
    wqT = nc.dram_tensor("wqT", [D, D], F32, kind="ExternalInput")
    wkT = nc.dram_tensor("wkT", [D, D], F32, kind="ExternalInput")
    wvT = nc.dram_tensor("wvT", [D, D], F32, kind="ExternalInput")
    woT = nc.dram_tensor("woT", [D, D], F32, kind="ExternalInput")
    b_q = nc.dram_tensor("b_q", [P, DT], F32, kind="ExternalInput")
    b_k = nc.dram_tensor("b_k", [P, DT], F32, kind="ExternalInput")
    b_v = nc.dram_tensor("b_v", [P, DT], F32, kind="ExternalInput")
    b_o = nc.dram_tensor("b_o", [P, DT], F32, kind="ExternalInput")
    wr = nc.dram_tensor("wr", [D, E], F32, kind="ExternalInput")
    wg = nc.dram_tensor("wg", [E, D, D], F32, kind="ExternalInput")
    wu = nc.dram_tensor("wu", [E, D, D], F32, kind="ExternalInput")
    wdn = nc.dram_tensor("wdn", [E, D, D], F32, kind="ExternalInput")
    out = nc.dram_tensor("out", [TOK, D], F32, kind="ExternalOutput")

    q_loc = nc.dram_tensor("q_loc", [D, TOK], BF16, kind="Internal")
    k_loc = nc.dram_tensor("k_loc", [TOK, D], BF16, kind="Internal")
    v_loc = nc.dram_tensor("v_loc", [TOK, D], BF16, kind="Internal")
    k_full = nc.dram_tensor("k_full", [S, D], BF16, kind="Internal")
    v_full = nc.dram_tensor("v_full", [S, D], BF16, kind="Internal")

    dbg = {}
    for name, shape, dt in (
        ("dbg_q", [P, H, TOK], BF16),
        ("dbg_kv", [TOK, 2 * D], BF16),
        ("dbg_lnt", [P, DT, TOK], BF16),
        ("dbg_ctx", [P, H, TOK], BF16),
        ("dbg_x1", [P, NT, D], F32),
        ("dbg_w8", [P, E, NT], F32),
        ("dbg_ln2", [P, DT, TOK], BF16),
    ):
        if name in _DBG:
            dbg[name] = nc.dram_tensor(name, shape, dt, kind="ExternalOutput")

    es = ExitStack()
    X_res = es.enter_context(nc.sbuf_tensor("X_res", [P, NT, D], F32))
    actT = es.enter_context(nc.sbuf_tensor("actT", [P, DT, TOK], BF16))
    scr = es.enter_context(nc.sbuf_tensor("scr", [P, 16384], BF16))
    bq_sb = es.enter_context(nc.sbuf_tensor("bq_sb", [P, DT], F32))
    bk_sb = es.enter_context(nc.sbuf_tensor("bk_sb", [P, DT], F32))
    bv_sb = es.enter_context(nc.sbuf_tensor("bv_sb", [P, DT], F32))
    bo_sb = es.enter_context(nc.sbuf_tensor("bo_sb", [P, DT], F32))
    ident = es.enter_context(nc.sbuf_tensor("ident", [P, P], BF16))
    c_sum = es.enter_context(nc.sbuf_tensor("c_sum", [P, NT], F32))
    c_ssq = es.enter_context(nc.sbuf_tensor("c_ssq", [P, NT], F32))
    c_negm = es.enter_context(nc.sbuf_tensor("c_negm", [P, NT], F32))
    c_rstd = es.enter_context(nc.sbuf_tensor("c_rstd", [P, NT], F32))
    c_t0 = es.enter_context(nc.sbuf_tensor("c_t0", [P, NT], F32))
    c_t1 = es.enter_context(nc.sbuf_tensor("c_t1", [P, NT], F32))
    wr_sb = es.enter_context(nc.sbuf_tensor("wr_sb", [P, DT, E], F32))
    wr_hi = es.enter_context(nc.sbuf_tensor("wr_hi", [P, DT, E], BF16))
    wr_lo = es.enter_context(nc.sbuf_tensor("wr_lo", [P, DT, E], BF16))
    W_sel = es.enter_context(nc.sbuf_tensor("W_sel", [P, E, NT], F32))

    junk = scr[:, 0:D]
    xst = [scr[:, D + i * D: D + (i + 1) * D] for i in range(2)]
    kvst = [scr[:, 3 * D + i * 512: 3 * D + (i + 1) * 512] for i in range(4)]
    PT4 = scr[:, :].rearrange("p (a b) -> p a b", a=NKT)
    hlT = scr[:, :].rearrange("p (a b) -> p a b", a=DT)

    def sem(name):
        return es.enter_context(nc.semaphore(name))

    def emit_G():
        gO = sem("gO")
        with nc.Block() as blk:
            @blk.sync
            def _(sy):
                sy.dma_start(out=_ap(out, 0, [[D, P], [P * D, NT], [1, D]]),
                             in_=X_res[:, :, :]).then_inc(gO, 16)
                sy.wait_ge(gO, 16)

    def strip_src(mat, oc, expert=None):
        base = oc * 512
        if expert is not None:
            base += expert * D * D
        return _ap(mat, base, [[D, P], [P * D, DT], [1, 512]])

    aId = sem("aId")
    aX = sem("aX")
    aB = sem("aB")
    aWs = [sem("aW0"), sem("aW1")]
    aS1 = sem("aS1")
    aV2 = sem("aV2")
    aS2 = sem("aS2")
    aV3 = sem("aV3")
    aTt = sem("aTt")
    aEv = sem("aEv")
    aPm = sem("aPm")
    aPe = sem("aPe")
    aKVs = [sem(f"aKV{i}") for i in range(4)]
    ccK = sem("ccK")
    ccV = sem("ccV")


    a_strips = ([strip_src(wkT, oc) for oc in range(4)]
                + [strip_src(wvT, oc) for oc in range(4)]
                + [strip_src(wqT, oc) for oc in range(4)])
    NG = 96

    def a_group(idx):
        if idx < 32:
            oc, tt = divmod(idx, 8)
            return ("k", oc, oc, tt, None)
        if idx < 64:
            oc, tt = divmod(idx - 32, 8)
            return ("v", 4 + oc, oc, tt, None)
        qi = idx - 64
        oc, r = divmod(qi, 8)
        os_, g2 = divmod(r, 2)
        return ("q", 8 + oc, oc, os_, g2)

    with (
        nc.sbuf_tensor("wbufA0", [P, DT, 512], BF16) as wbA0,
        nc.sbuf_tensor("wbufA1", [P, DT, 512], BF16) as wbA1,
        nc.psum_tensor("tpA", [P, 8, P], BF16) as tpA,
        nc.psum_tensor("tpB", [P, 8, P], BF16) as tpB,
        nc.psum_tensor("tpC", [P, 8, P], BF16) as tpC,
        nc.psum_tensor("tpD", [P, 8, P], BF16) as tpD,
        nc.psum_tensor("pm0", [P, 512], F32) as pm0,
        nc.psum_tensor("pm1", [P, 512], F32) as pm1,
        nc.psum_tensor("pm2", [P, 512], F32) as pm2,
        nc.psum_tensor("pm3", [P, 512], F32) as pm3,
        nc.Block() as blk,
    ):
        wbuf = [wbA0, wbA1]
        tp = [[tpA, tpB], [tpC, tpD]]
        pmm = [pm0, pm1, pm2, pm3]

        @blk.sync
        def _(sy):
            sy.dma_start(out=X_res[:, :, :],
                         in_=_ap(x_own, 0, [[D, P], [P * D, NT], [1, D]])
                         ).then_inc(aX, 16)
            for bsrc, bdst in ((b_q, bq_sb), (b_k, bk_sb), (b_v, bv_sb),
                               (b_o, bo_sb)):
                sy.dma_start(out=bdst[:, :], in_=bsrc[:, :]).then_inc(aB, 16)
            sy.dma_start(out=wr_sb[:, :, :],
                         in_=_ap(wr, 0, [[E, P], [E * P, DT], [1, E]])
                         ).then_inc(aB, 16)
            for idx in range(96):
                kind, _, oc, a3, g2 = a_group(idx)
                sy.wait_ge(aPe, idx + 1)
                if kind in ("k", "v"):
                    dst = k_loc if kind == "k" else v_loc
                    out_ap = dst[a3 * P:(a3 + 1) * P,
                                 oc * 512:(oc + 1) * 512]
                else:
                    ot = oc * 4 + a3
                    out_ap = _ap(q_loc, ot * P * TOK + g2 * 512,
                                 [[TOK, P], [1, 512]])
                sy.dma_start(out=out_ap,
                             in_=kvst[idx % 4]).then_inc(aKVs[idx % 4], 16)
            sy.wait_ge(aB, 80)

        @blk.gpsimd
        def _(g):
            g = _SER(g)
            g.memset(ident[:, :], 0.0).then_inc(aId, 1)
            g.wait_ge(aId, 1)
            g.affine_select(out=ident[:, :], in_=ident[:, :],
                            compare_op=ALU.not_equal, fill=1.0, base=0,
                            pattern=[[-1, P]], channel_multiplier=1
                            ).then_inc(aId, 1)
            for i, src in enumerate(a_strips):
                if i == 8:
                    for q in range(4):
                        g.wait_ge(aKVs[q], 16 * 8)
                    g.collective_compute(
                        "AllGather", ALU.bypass,
                        replica_groups=[[0, 1, 2, 3], [4, 5, 6, 7]],
                        ins=[k_loc.ap().opt()], outs=[k_full.ap().opt()],
                    ).then_inc(ccK)
                if i >= 2:
                    g.wait_ge(aPe, (i - 2) * 8 + 8)
                g.dma_start(out=wbuf[i % 2][:, :, :], in_=src
                            ).then_inc(aWs[i % 2], 16)
            for q in range(4):
                g.wait_ge(aKVs[q], 16 * 16)
            g.collective_compute(
                "AllGather", ALU.bypass,
                replica_groups=[[0, 1, 2, 3], [4, 5, 6, 7]],
                ins=[v_loc.ap().opt()], outs=[v_full.ap().opt()],
            ).then_inc(ccV)

        @blk.scalar
        def _(s):
            s = _SER(s)
            s.wait_ge(aX, 16)
            for j in range(NT):
                s.activation(out=junk, in_=X_res[:, j, :], func=AF.Square,
                             accum_out=c_ssq[:, j:j + 1]).then_inc(aS1, 1)
            def a_evac(j):
                for half in range(2):
                    s.wait_ge(aTt, 2 * j + half + 1)
                    s.copy(
                        actT[:, half * 8:(half + 1) * 8, j * P:(j + 1) * P],
                        tp[j % 2][half][:, :, :]).then_inc(aEv, 1)
            for j in range(NT):
                s.wait_ge(aV2, j + 1)
                s.activation(out=c_rstd[:, j:j + 1], in_=c_t1[:, j:j + 1],
                             func=AF.Sqrt).then_inc(aS2, 1)
                if j >= 2:
                    a_evac(j - 2)
            a_evac(NT - 2)
            a_evac(NT - 1)

        @blk.vector
        def _(v):
            v = _SER(v)
            v.wait_ge(aX, 16)
            for j in range(NT):
                v.tensor_reduce(c_sum[:, j:j + 1], X_res[:, j, :], AX.X,
                                ALU.add)
                v.wait_ge(aS1, j + 1)
                v.tensor_scalar(c_t0[:, j:j + 1], c_sum[:, j:j + 1],
                                1.0 / D, None, ALU.mult)
                v.tensor_scalar(c_negm[:, j:j + 1], c_t0[:, j:j + 1],
                                -1.0, None, ALU.mult)
                v.tensor_scalar(c_t1[:, j:j + 1], c_ssq[:, j:j + 1],
                                1.0 / D, None, ALU.mult)
                v.scalar_tensor_tensor(c_t1[:, j:j + 1], c_t0[:, j:j + 1],
                                       c_negm[:, j:j + 1], c_t1[:, j:j + 1],
                                       ALU.mult, ALU.add)
                v.tensor_scalar(c_t1[:, j:j + 1], c_t1[:, j:j + 1],
                                EPS, None, ALU.add)
                v.reciprocal(c_t1[:, j:j + 1], c_t1[:, j:j + 1]
                             ).then_inc(aV2, 1)
                v.wait_ge(aS2, j + 1)
                if j >= 2:
                    v.wait_ge(aTt, 2 * (j - 2) + 2)
                v.tensor_scalar(xst[j % 2], X_res[:, j, :],
                                c_negm[:, j:j + 1], c_rstd[:, j:j + 1],
                                ALU.add, ALU.mult).then_inc(aV3, 1)
            for idx in range(NG):
                kind, _, oc, a3, g2 = a_group(idx)
                v.wait_ge(aPm, idx + 1)
                b = idx % 4
                if kind in ("k", "v"):
                    if idx >= 4:
                        v.wait_ge(aKVs[idx % 4], 16 * (idx // 4))
                    v.tensor_copy(kvst[idx % 4], pmm[b][:, :]).then_inc(aPe, 1)
                else:
                    ot = oc * 4 + a3
                    if idx == 64:
                        v.wait_ge(aB, 80)
                    if idx >= 4:
                        v.wait_ge(aKVs[idx % 4], 16 * (idx // 4))
                    v.tensor_scalar(kvst[idx % 4],
                                    pmm[b][:, :], bq_sb[:, ot:ot + 1],
                                    INV_C, ALU.add, ALU.mult).then_inc(aPe, 1)

        @blk.tensor
        def _(t):
            t.wait_ge(aId, 2)
            for j in range(NT):
                t.wait_ge(aV3, j + 1)
                if j >= 2:
                    t.wait_ge(aEv, 2 * (j - 2) + 2)
                for dt in range(DT):
                    bank = tp[j % 2][dt // 8]
                    ins = t.transpose(bank[:, dt % 8, :],
                                      xst[j % 2][:, dt * P:(dt + 1) * P],
                                      ident[:, :])
                    if dt % 8 == 7:
                        ins.then_inc(aTt, 1)
            for idx in range(NG):
                kind, strip, oc, a3, g2 = a_group(idx)
                t.wait_ge(aWs[strip % 2], 16 * (strip // 2 + 1))
                if kind in ("k", "v"):
                    t.wait_ge(aEv, 2 * a3 + 2)
                else:
                    t.wait_ge(aEv, 2 * NT)
                if idx >= 4:
                    t.wait_ge(aPe, idx - 3)
                b = idx % 4
                for dt in range(DT):
                    if kind in ("k", "v"):
                        ins = t.matmul(pmm[b][:, :],
                                       actT[:, dt, a3 * P:(a3 + 1) * P],
                                       wbuf[strip % 2][:, dt, :],
                                       start=(dt == 0), stop=(dt == DT - 1))
                    else:
                        ins = t.matmul(
                            pmm[b][:, :],
                            wbuf[strip % 2][:, dt, a3 * P:(a3 + 1) * P],
                            actT[:, dt, g2 * 512:(g2 + 1) * 512],
                            start=(dt == 0), stop=(dt == DT - 1))
                ins.then_inc(aPm, 1)

    if {"dbg_kv", "dbg_q", "dbg_lnt"} & set(dbg):
        dS = sem("dS")
        with nc.Block() as blk:
            @blk.sync
            def _(sy):
                n = 0
                for key, src in (("dbg_kv", None),
                                 ("dbg_lnt", actT[:, :, :])):
                    if key == "dbg_kv" and key in dbg:
                        sy.dma_start(out=dbg[key].ap()[:, 0:D], in_=k_loc.ap()
                                     ).then_inc(dS, 16)
                        sy.dma_start(out=dbg[key].ap()[:, D:2 * D],
                                     in_=v_loc.ap()).then_inc(dS, 16)
                        n += 32
                        continue
                    if key in dbg:
                        sy.dma_start(out=dbg[key].ap(), in_=src
                                     ).then_inc(dS, 16)
                        n += 16
                sy.wait_ge(dS, n)

    if _STOP <= "B":
        emit_G()
        es.close()
        return nc, dbg
    cKHs = [sem("cKH0"), sem("cKH1")]
    cKb = sem("cKb")
    cVHs = [sem("cVH0"), sem("cVH1")]
    cSm = sem("cSm")
    cMx = sem("cMx")
    cMl = sem("cMl")
    cEx = sem("cEx")
    cFa = sem("cFa")
    cNm = sem("cNm")
    cPTs = [sem(f"cPT{i}") for i in range(4)]
    cPV = sem("cPV")
    cCx = sem("cCx")

    cQHs = [sem("cQH0"), sem("cQH1")]
    with (
        nc.sbuf_tensor("Q_h", [P, 2, TOK], BF16) as Q_h,
        nc.sbuf_tensor("K_h", [P, 2, S], BF16) as K_h,
        nc.sbuf_tensor("V_h", [P, 2, NKT, P], BF16) as V_h,
        nc.sbuf_tensor("P_buf", [P, 2, S], BF16) as P_buf,
        nc.sbuf_tensor("negm8", [P, 2, 8], F32) as negm8,
        nc.sbuf_tensor("sum8", [P, 2, 8], F32) as sum8,
        nc.sbuf_tensor("fac8", [P, 2, 8], F32) as fac8,
        nc.sbuf_tensor("c_mall", [P, 2], F32) as c_mall,
        nc.sbuf_tensor("c_sall", [P, 2], F32) as c_sall,
        nc.sbuf_tensor("junk8", [P, 8], F32) as junk8,
        nc.psum_tensor("ps_s", [P, 6, 512], F32) as ps_s,
        nc.psum_tensor("ps_c0", [P, 512], F32) as ps_c0,
        nc.psum_tensor("ps_c1", [P, 512], F32) as ps_c1,
        nc.Block() as blk,
    ):
        ps_c = [ps_c0, ps_c1]

        @blk.sync
        def _(sy):
            def k_load(h):
                if h >= H:
                    return
                if h == 0:
                    sy.wait_ge(ccK, 1)
                if h >= 2:
                    sy.wait_ge(cSm, 64 * (h - 1))
                sy.dma_start_transpose(
                    out=K_h[:, h % 2, :],
                    in_=k_full[:, h * P:(h + 1) * P]).then_inc(cKHs[h % 2], 16)

            for q in range(4):
                sy.wait_ge(aKVs[q], 16 * 24)
            k_load(0)
            k_load(1)
            for gq in range(H * NT):
                sl, gb = gq % 4, gq // 4
                sy.wait_ge(cNm, gq * 4 + 4)
                sy.wait_ge(cPV, gb)
                sy.dma_start_transpose(
                    out=PT4[:, :, sl * P:(sl + 1) * P],
                    in_=P_buf[:, gq % 2, :].rearrange(
                        "q (kt p) -> q kt p", kt=NKT)
                    ).then_inc(cPTs[sl], 16)
                if gq % 8 == 5:
                    k_load(gq // 8 + 2)

        @blk.scalar
        def _(s):
            s = _SER(s)
            for q in range(4):
                s.wait_ge(aKVs[q], 16 * 24)
            for h in range(H):
                if h >= 2:
                    s.wait_ge(cSm, 64 * (h - 1))
                s.dma_start(out=Q_h[:, h % 2, :],
                            in_=q_loc[h * P:(h + 1) * P, :]
                            ).then_inc(cQHs[h % 2], 16)
                if h == 0:
                    s.wait_ge(ccV, 1)
                if h >= 2:
                    s.wait_ge(cPV, 2 * h - 2)
                s.dma_start(
                    out=V_h[:, h % 2, :, :],
                    in_=_ap(v_full, h * P, [[D, P], [P * D, NKT], [1, P]])
                    ).then_inc(cVHs[h % 2], 16)
                for jq in range(NT):
                    gq = h * NT + jq
                    jr = jq % 2
                    if gq >= 2:
                        psl = (gq - 2) % 4
                        s.wait_ge(cPTs[psl], 16 * ((gq - 2) // 4 + 1))
                    if gq >= 2:
                        s.wait_ge(cFa, gq - 1)
                    for kp in range(4):
                        pl = gq * 4 + kp
                        b0 = 2 * (pl % 3)
                        s.wait_ge(cMx, pl + 1)
                        s.activation(out=P_buf[:, gq % 2,
                                                kp * 1024:(kp + 1) * 1024],
                                     in_=ps_s[:, b0:b0 + 2, :], func=AF.Exp,
                                     bias=negm8[:, jr, kp:kp + 1], scale=1.0,
                                     accum_out=sum8[:, jr, kp:kp + 1]
                                     ).then_inc(cEx, 1)
                    s.wait_ge(cMl, gq + 1)
                    s.activation(out=fac8[:, jr, 0:4], in_=negm8[:, jr, 0:4],
                                 func=AF.Exp, bias=c_mall[:, jr:jr + 1],
                                 scale=-1.0).then_inc(cEx, 1)

        @blk.vector
        def _(v):
            v = _SER(v)

            def v_ctx_evac(kb):
                v.wait_ge(cPV, kb + 1)
                v.tensor_scalar(actT[:, kb // 2,
                                     (kb % 2) * 512:(kb % 2) * 512 + 512],
                                ps_c[kb % 2][:, :],
                                bv_sb[:, kb // 2:kb // 2 + 1],
                                None, ALU.add).then_inc(cCx, 1)

            def v_stats_norm(gq):
                jr = (gq % NT) % 2
                v.wait_ge(cEx, gq * 5 + 5)
                v.tensor_mul(junk8[:, 0:4], sum8[:, jr, 0:4],
                             fac8[:, jr, 0:4])
                v.tensor_reduce(c_sall[:, jr:jr + 1], junk8[:, 0:4],
                                AX.X, ALU.add)
                v.reciprocal(c_sall[:, jr:jr + 1], c_sall[:, jr:jr + 1])
                v.tensor_scalar(fac8[:, jr, 0:4], fac8[:, jr, 0:4],
                                c_sall[:, jr:jr + 1], None, ALU.mult
                                ).then_inc(cFa, 1)
                for kp in range(4):
                    v.tensor_scalar(P_buf[:, gq % 2,
                                          kp * 1024:(kp + 1) * 1024],
                                    P_buf[:, gq % 2,
                                          kp * 1024:(kp + 1) * 1024],
                                    fac8[:, jr, kp:kp + 1], None,
                                    ALU.mult).then_inc(cNm, 1)

            prev = None
            for h in range(H):
                v.wait_ge(cKHs[h % 2], 16 * (h // 2 + 1))
                if h == 0:
                    v.wait_ge(aB, 80)
                v.tensor_scalar(K_h[:, h % 2, :], K_h[:, h % 2, :],
                                bk_sb[:, h:h + 1], None, ALU.add
                                ).then_inc(cKb, 1)
                for jq in range(NT):
                    gq = h * NT + jq
                    jr = jq % 2
                    for kp in range(4):
                        v.wait_ge(cSm, gq * 8 + 2 * kp + 2)
                        pl = gq * 4 + kp
                        b0 = 2 * (pl % 3)
                        v.tensor_reduce(negm8[:, jr, kp:kp + 1],
                                        ps_s[:, b0:b0 + 2, :],
                                        AX.XY, ALU.max,
                                        negate=True).then_inc(cMx, 1)
                    v.tensor_reduce(c_mall[:, jr:jr + 1], negm8[:, jr, 0:4],
                                    AX.X, ALU.min).then_inc(cMl, 1)
                    if prev is not None:
                        v_stats_norm(prev)
                    prev = gq
                    if gq >= 5 and (gq - 5) % 4 == 0:
                        v_ctx_evac((gq - 5) // 4)
            v_stats_norm(prev)
            v_ctx_evac(30)
            v_ctx_evac(31)

        @blk.tensor
        def _(t):
            def t_pv(gb):
                hh = gb // 2
                for q in range(4):
                    t.wait_ge(cPTs[q], 16 * (gb + 1))
                if gb >= 2:
                    t.wait_ge(cCx, gb - 1)
                for kt in range(NKT):
                    ins = t.matmul(ps_c[gb % 2][:, :],
                                   V_h[:, hh % 2, kt, :],
                                   PT4[:, kt, :],
                                   start=(kt == 0),
                                   stop=(kt == NKT - 1))
                ins.then_inc(cPV, 1)

            for h in range(H):
                t.wait_ge(cKb, h + 1)
                t.wait_ge(cQHs[h % 2], 16 * (h // 2 + 1))
                t.wait_ge(cVHs[h % 2], 16 * (h // 2 + 1))
                for jq in range(NT):
                    gq = h * NT + jq
                    for kc in range(8):
                        pl = gq * 4 + kc // 2
                        if kc % 2 == 0 and pl >= 3:
                            pg, pp = divmod(pl - 3, 4)
                            t.wait_ge(cEx, pg * 5 + pp + 1)
                        bank = (2 * (pl % 3)) + (kc % 2)
                        t.matmul(ps_s[:, bank, :],
                                 Q_h[:, h % 2, jq * P:(jq + 1) * P],
                                 K_h[:, h % 2, kc * 512:(kc + 1) * 512],
                                 start=True, stop=True).then_inc(cSm, 1)
                    if gq % 4 == 0 and gq >= 4:
                        t_pv(gq // 4 - 1)
            t_pv(31)


    if "dbg_ctx" in dbg:
        dC = sem("dC")
        with nc.Block() as blk:
            @blk.sync
            def _(sy):
                sy.dma_start(out=dbg["dbg_ctx"].ap(), in_=actT[:, :, :]
                             ).then_inc(dC, 16)
                sy.wait_ge(dC, 16)

    if _STOP <= "C":
        emit_G()
        es.close()
        return nc, dbg

    dWs = [sem("dW0"), sem("dW1")]
    dMM = sem("dMM")
    dEv = sem("dEv")
    dTp = sem("dTp")
    dAdd = sem("dAdd")

    with (
        nc.sbuf_tensor("wbufD0", [P, DT, 512], BF16) as wbD0,
        nc.sbuf_tensor("wbufD1", [P, DT, 512], BF16) as wbD1,
        nc.sbuf_tensor("stageO", [P, 2, 512], BF16) as stageO,
        nc.psum_tensor("pd0", [P, 512], F32) as pd0,
        nc.psum_tensor("pd1", [P, 512], F32) as pd1,
        nc.psum_tensor("ptd0", [P, 4, P], BF16) as ptd0,
        nc.psum_tensor("ptd1", [P, 4, P], BF16) as ptd1,
        nc.Block() as blk,
    ):
        wbufD = [wbD0, wbD1]
        pd = [pd0, pd1]
        ptd = [ptd0, ptd1]

        def d_tp(t, pidx):
            t.wait_ge(dEv, pidx + 1)
            if pidx >= 2:
                t.wait_ge(dAdd, pidx - 1)
            for k in range(4):
                ins = t.transpose(ptd[pidx % 2][:, k, :],
                                  stageO[:, pidx % 2, k * P:(k + 1) * P],
                                  ident[:, :])
            ins.then_inc(dTp, 1)

        def d_add(v, pidx):
            poc, pr = divmod(pidx, 8)
            pos, pg2 = divmod(pr, 2)
            pot = poc * 4 + pos
            v.wait_ge(dTp, pidx + 1)
            v.tensor_tensor(
                X_res[:, pg2 * 4:pg2 * 4 + 4, pot * P:(pot + 1) * P],
                ptd[pidx % 2][:, :, :],
                X_res[:, pg2 * 4:pg2 * 4 + 4, pot * P:(pot + 1) * P],
                ALU.add).then_inc(dAdd, 1)

        @blk.gpsimd
        def _(g):
            g = _SER(g)
            for oc in range(4):
                if oc >= 2:
                    g.wait_ge(dEv, (oc - 2) * 8 + 8)
                g.dma_start(out=wbufD[oc % 2][:, :, :],
                            in_=strip_src(woT, oc)).then_inc(dWs[oc % 2], 16)

        @blk.tensor
        def _(t):
            for idx in range(32):
                oc, r = divmod(idx, 8)
                os_, g2 = divmod(r, 2)
                t.wait_ge(dWs[oc % 2], 16 * (oc // 2 + 1))
                if idx >= 2:
                    t.wait_ge(dEv, idx - 1)
                for ct in range(DT):
                    ins = t.matmul(pd[idx % 2][:, :],
                                   wbufD[oc % 2][:, ct, os_ * P:(os_ + 1) * P],
                                   actT[:, ct, g2 * 512:(g2 + 1) * 512],
                                   start=(ct == 0), stop=(ct == DT - 1))
                ins.then_inc(dMM, 1)
                if idx >= 1:
                    d_tp(t, idx - 1)
            d_tp(t, 31)

        @blk.vector
        def _(v):
            v = _SER(v)
            for idx in range(32):
                oc, r = divmod(idx, 8)
                os_, g2 = divmod(r, 2)
                ot = oc * 4 + os_
                v.wait_ge(dMM, idx + 1)
                if idx >= 2:
                    v.wait_ge(dTp, idx - 1)
                v.tensor_scalar(stageO[:, idx % 2, :], pd[idx % 2][:, :],
                                bo_sb[:, ot:ot + 1], None, ALU.add
                                ).then_inc(dEv, 1)
                if idx >= 1:
                    d_add(v, idx - 1)
            d_add(v, 31)

    if "dbg_x1" in dbg:
        dX = sem("dX")
        with nc.Block() as blk:
            @blk.sync
            def _(sy):
                sy.dma_start(out=dbg["dbg_x1"].ap(), in_=X_res[:, :, :]
                             ).then_inc(dX, 16)
                sy.wait_ge(dX, 16)

    if _STOP <= "D":
        emit_G()
        es.close()
        return nc, dbg

    eS1 = sem("eS1")
    eV2 = sem("eV2")
    eS2 = sem("eS2")
    eV3 = sem("eV3")
    eTt = sem("eTt")
    eEv = sem("eEv")
    eRm = sem("eRm")
    eRe = sem("eRe")
    eRx = sem("eRx")
    eRw = sem("eRw")

    with (
        nc.sbuf_tensor("stgf", [P, 2, D], F32) as stgf,
        nc.sbuf_tensor("stghi", [P, 2, D], BF16) as stghi,
        nc.sbuf_tensor("stglo", [P, 2, D], BF16) as stglo,
        nc.sbuf_tensor("junkE", [P, D], BF16) as junkE,
        nc.sbuf_tensor("lg_sb", [P, NT, E], F32) as lg_sb,
        nc.sbuf_tensor("e8", [P, E], F32) as e8,
        nc.sbuf_tensor("r_m1", [P, 1], F32) as r_m1,
        nc.sbuf_tensor("r_nm1", [P, 1], F32) as r_nm1,
        nc.sbuf_tensor("r_m2", [P, 1], F32) as r_m2,
        nc.sbuf_tensor("r_msk", [P, E], F32) as r_msk,
        nc.sbuf_tensor("r_sel", [P, E], F32) as r_sel,
        nc.sbuf_tensor("r_wun", [P, E], F32) as r_wun,
        nc.sbuf_tensor("r_ss", [P, 1], F32) as r_ss,
        nc.psum_tensor("tpE0", [P, 8, P], BF16) as tpE0,
        nc.psum_tensor("tpE1", [P, 8, P], BF16) as tpE1,
        nc.psum_tensor("tpE2", [P, 8, P], BF16) as tpE2,
        nc.psum_tensor("tpE3", [P, 8, P], BF16) as tpE3,
        nc.psum_tensor("prt", [P, E], F32) as prt,
        nc.Block() as blk,
    ):
        tpE = [tpE0, tpE1, tpE2, tpE3]

        @blk.scalar
        def _(s):
            s = _SER(s)
            for j in range(NT):
                s.activation(out=junkE[:, :], in_=X_res[:, j, :], func=AF.Square,
                             accum_out=c_ssq[:, j:j + 1]).then_inc(eS1, 1)
            def e_evac(j):
                for q in range(4):
                    s.wait_ge(eTt, 4 * j + q + 1)
                    dst = actT if q < 2 else hlT
                    half = q % 2
                    s.copy(
                        dst[:, half * 8:(half + 1) * 8, j * P:(j + 1) * P],
                        tpE[q][:, :, :]).then_inc(eEv, 1)
            for j in range(NT):
                s.wait_ge(eV2, j + 1)
                s.activation(out=c_rstd[:, j:j + 1], in_=c_t1[:, j:j + 1],
                             func=AF.Sqrt).then_inc(eS2, 1)
                if j >= 2:
                    e_evac(j - 2)
            e_evac(NT - 2)
            e_evac(NT - 1)
            for j in range(NT):
                s.wait_ge(eRe, 3 * j + 3)
                s.activation(out=e8[:, :], in_=lg_sb[:, j, :], func=AF.Exp,
                             bias=r_nm1[:, :], scale=1.0).then_inc(eRx, 1)

        @blk.vector
        def _(v):
            v = _SER(v)
            for j in range(NT):
                v.tensor_reduce(c_sum[:, j:j + 1], X_res[:, j, :], AX.X,
                                ALU.add)
                v.wait_ge(eS1, j + 1)
                v.tensor_scalar(c_t0[:, j:j + 1], c_sum[:, j:j + 1],
                                1.0 / D, None, ALU.mult)
                v.tensor_scalar(c_negm[:, j:j + 1], c_t0[:, j:j + 1],
                                -1.0, None, ALU.mult)
                v.tensor_scalar(c_t1[:, j:j + 1], c_ssq[:, j:j + 1],
                                1.0 / D, None, ALU.mult)
                v.scalar_tensor_tensor(c_t1[:, j:j + 1], c_t0[:, j:j + 1],
                                       c_negm[:, j:j + 1], c_t1[:, j:j + 1],
                                       ALU.mult, ALU.add)
                v.tensor_scalar(c_t1[:, j:j + 1], c_t1[:, j:j + 1],
                                EPS, None, ALU.add)
                v.reciprocal(c_t1[:, j:j + 1], c_t1[:, j:j + 1]
                             ).then_inc(eV2, 1)
                v.wait_ge(eS2, j + 1)
                if j >= 2:
                    v.wait_ge(eTt, 4 * (j - 2) + 4)
                v.tensor_scalar(stgf[:, j % 2, :], X_res[:, j, :],
                                c_negm[:, j:j + 1], c_rstd[:, j:j + 1],
                                ALU.add, ALU.mult)
                v.tensor_copy(stghi[:, j % 2, :], stgf[:, j % 2, :]
                              ).then_inc(eV3, 1)
                v.tensor_sub(stglo[:, j % 2, :], stgf[:, j % 2, :],
                             stghi[:, j % 2, :]).then_inc(eV3, 1)
            v.tensor_copy(wr_hi[:, :, :], wr_sb[:, :, :])
            v.tensor_sub(wr_lo[:, :, :], wr_sb[:, :, :], wr_hi[:, :, :]
                         ).then_inc(eV3, 1)
            for j in range(NT):
                v.wait_ge(eRm, j + 1)
                v.tensor_copy(lg_sb[:, j, :], prt[:, :]).then_inc(eRe, 1)
                v.tensor_reduce(r_m1[:, :], lg_sb[:, j, :], AX.X, ALU.max)
                if j >= 1:
                    v.wait_ge(eRx, j)
                v.tensor_scalar(r_nm1[:, :], r_m1[:, :], -1.0, None,
                                ALU.mult).then_inc(eRe, 1)
                v.tensor_scalar(r_msk[:, :], lg_sb[:, j, :], r_m1[:, :],
                                -1e30, ALU.is_ge, ALU.mult)
                v.tensor_add(r_msk[:, :], lg_sb[:, j, :], r_msk[:, :])
                v.tensor_reduce(r_m2[:, :], r_msk[:, :], AX.X, ALU.max
                                ).then_inc(eRe, 1)
                v.wait_ge(eRx, j + 1)
                v.tensor_scalar(r_sel[:, :], lg_sb[:, j, :], r_m2[:, :],
                                None, ALU.is_ge)
                v.tensor_mul(r_wun[:, :], e8[:, :], r_sel[:, :])
                v.tensor_reduce(r_ss[:, :], r_wun[:, :], AX.X, ALU.add)
                v.reciprocal(r_ss[:, :], r_ss[:, :])
                v.tensor_scalar(W_sel[:, :, j], r_wun[:, :], r_ss[:, :],
                                None, ALU.mult).then_inc(eRw, 1)

        @blk.tensor
        def _(t):
            for j in range(NT):
                for pas in range(2):
                    t.wait_ge(eV3, 2 * j + pas + 1)
                    if j >= 1:
                        t.wait_ge(eEv, 4 * (j - 1) + 2 * pas + 2)
                    src = stghi if pas == 0 else stglo
                    for dt in range(DT):
                        bank = tpE[2 * pas + dt // 8]
                        ins = t.transpose(bank[:, dt % 8, :],
                                          src[:, j % 2, dt * P:(dt + 1) * P],
                                          ident[:, :])
                        if dt % 8 == 7:
                            ins.then_inc(eTt, 1)
            t.wait_ge(eV3, 2 * NT + 1)
            t.wait_ge(eEv, 4 * NT)
            for j in range(NT):
                if j >= 1:
                    t.wait_ge(eRe, 3 * (j - 1) + 1)
                nmm = 0
                for lhs_src, rhs_src in ((actT, wr_hi), (actT, wr_lo),
                                         (hlT, wr_hi)):
                    for dt in range(DT):
                        ins = t.matmul(prt[:, :],
                                       lhs_src[:, dt, j * P:(j + 1) * P],
                                       rhs_src[:, dt, :],
                                       start=(nmm == 0), stop=(nmm == 47))
                        nmm += 1
                ins.then_inc(eRm, 1)

    if {"dbg_w8", "dbg_ln2"} & set(dbg):
        dW8 = sem("dW8")
        with nc.Block() as blk:
            @blk.sync
            def _(sy):
                n = 0
                for key, src in (("dbg_w8", W_sel[:, :, :]),
                                 ("dbg_ln2", actT[:, :, :])):
                    if key in dbg:
                        sy.dma_start(out=dbg[key].ap(), in_=src
                                     ).then_inc(dW8, 16)
                        n += 16
                sy.wait_ge(dW8, n)

    if _STOP <= "E":
        emit_G()
        es.close()
        return nc, dbg

    fWs = [sem("fW0"), sem("fW1")]
    fMM = sem("fMM")
    fEvG = sem("fEvG")
    fEvV = sem("fEvV")
    fTp = sem("fTp")
    fAdd = sem("fAdd")

    f_strips = []
    for i in range(E):
        for fc in range(4):
            f_strips.append(strip_src(wg, fc, i))
            f_strips.append(strip_src(wu, fc, i))
        for oc in range(4):
            f_strips.append(strip_src(wdn, oc, i))

    fgroups = []
    for i in range(E):
        for fc in range(4):
            for fs in range(4):
                for g2 in range(2):
                    fgroups.append(("g", i, fc, fs, g2, i * 12 + 2 * fc))
            for fs in range(4):
                for g2 in range(2):
                    fgroups.append(("u", i, fc, fs, g2, i * 12 + 2 * fc + 1))
        for oc in range(4):
            for os_ in range(4):
                for g2 in range(2):
                    fgroups.append(("d", i, oc, os_, g2, i * 12 + 8 + oc))
    NFG = len(fgroups)
    evac_of = []
    nG = nV = 0
    d_index = {}
    g_evac_idx = {}
    u_evac_idx = {}
    ndi = 0
    for m, (kind, i, c, s_, g2, si) in enumerate(fgroups):
        if kind == "g":
            nG += 1
            evac_of.append(("G", nG))
            g_evac_idx[(i, c, s_, g2)] = nG
        else:
            nV += 1
            evac_of.append(("V", nV))
            if kind == "u":
                u_evac_idx[(i, c, s_, g2)] = nV
            else:
                d_index[m] = ndi
                ndi += 1
    NDG = ndi
    d_list = [m for m in range(NFG) if fgroups[m][0] == "d"]
    strip_last = {}
    for m, grp in enumerate(fgroups):
        strip_last[grp[5]] = m

    with (
        nc.sbuf_tensor("wbufF0", [P, DT, 512], BF16) as wbF0,
        nc.sbuf_tensor("wbufF1", [P, DT, 512], BF16) as wbF1,
        nc.sbuf_tensor("sg_buf", [P, 4, TOK], BF16) as sg_buf,
        nc.sbuf_tensor("stageE", [P, 2, 512], BF16) as stageE,
        nc.psum_tensor("pf0", [P, 512], F32) as pf0,
        nc.psum_tensor("pf1", [P, 512], F32) as pf1,
        nc.psum_tensor("pf2", [P, 512], F32) as pf2,
        nc.psum_tensor("pf3", [P, 512], F32) as pf3,
        nc.psum_tensor("ptf0", [P, 4, P], BF16) as ptf0,
        nc.psum_tensor("ptf1", [P, 4, P], BF16) as ptf1,
        nc.Block() as blk,
    ):
        wbufF = [wbF0, wbF1]
        pf = [pf0, pf1, pf2, pf3]
        ptf = [ptf0, ptf1]

        def f_tp(t, di):
            m = d_list[di]
            _, ei = evac_of[m]
            t.wait_ge(fEvV, ei)
            if di >= 2:
                t.wait_ge(fAdd, 4 * (di - 2) + 4)
            for k in range(4):
                ins = t.transpose(ptf[di % 2][:, k, :],
                                  stageE[:, di % 2, k * P:(k + 1) * P],
                                  ident[:, :])
            ins.then_inc(fTp, 1)

        def f_add(v, di):
            m = d_list[di]
            _, i, oc, os_, g2, _ = fgroups[m]
            ot = oc * 4 + os_
            v.wait_ge(fTp, di + 1)
            for k in range(4):
                j = g2 * 4 + k
                v.scalar_tensor_tensor(
                    X_res[:, j, ot * P:(ot + 1) * P],
                    ptf[di % 2][:, k, :], W_sel[:, i, j:j + 1],
                    X_res[:, j, ot * P:(ot + 1) * P],
                    ALU.mult, ALU.add).then_inc(fAdd, 1)

        @blk.gpsimd
        def _(g):
            g = _SER(g)
            for si, src in enumerate(f_strips):
                if si >= 2:
                    ek, ei = evac_of[strip_last[si - 2]]
                    g.wait_ge(fEvG if ek == "G" else fEvV, ei)
                g.dma_start(out=wbufF[si % 2][:, :, :], in_=src
                            ).then_inc(fWs[si % 2], 16)

        @blk.tensor
        def _(t):
            for m, (kind, i, c, s_, g2, si) in enumerate(fgroups):
                t.wait_ge(fWs[si % 2], 16 * (si // 2 + 1))
                if m >= 4:
                    ek, ei = evac_of[m - 4]
                    t.wait_ge(fEvG if ek == "G" else fEvV, ei)
                if kind == "d" and c == 0 and s_ == 0 and g2 == 0:
                    t.wait_ge(fEvV, u_evac_idx[(i, 3, 3, 1)])
                b = m % 4
                rhs_base = actT if kind in ("g", "u") else hlT
                for dt in range(DT):
                    ins = t.matmul(pf[b][:, :],
                                   wbufF[si % 2][:, dt, s_ * P:(s_ + 1) * P],
                                   rhs_base[:, dt, g2 * 512:(g2 + 1) * 512],
                                   start=(dt == 0), stop=(dt == DT - 1))
                ins.then_inc(fMM, 1)
                if kind == "d":
                    di = d_index[m]
                    if di >= 1:
                        f_tp(t, di - 1)
            f_tp(t, NDG - 1)

        @blk.scalar
        def _(s):
            s = _SER(s)
            for m, (kind, i, c, s_, g2, si) in enumerate(fgroups):
                if kind != "g":
                    continue
                s.wait_ge(fMM, m + 1)
                if c >= 1:
                    s.wait_ge(fEvV, u_evac_idx[(i, c - 1, s_, g2)])
                elif i >= 1:
                    s.wait_ge(fEvV, u_evac_idx[(i - 1, 3, s_, g2)])
                s.activation(out=sg_buf[:, s_, g2 * 512:(g2 + 1) * 512],
                             in_=pf[m % 4][:, :], func=AF.Silu
                             ).then_inc(fEvG, 1)

        @blk.vector
        def _(v):
            v = _SER(v)
            for m, (kind, i, c, s_, g2, si) in enumerate(fgroups):
                if kind == "g":
                    continue
                v.wait_ge(fMM, m + 1)
                if kind == "u":
                    v.wait_ge(fEvG, g_evac_idx[(i, c, s_, g2)])
                    v.tensor_tensor(
                        hlT[:, c * 4 + s_, g2 * 512:(g2 + 1) * 512],
                        sg_buf[:, s_, g2 * 512:(g2 + 1) * 512],
                        pf[m % 4][:, :], ALU.mult).then_inc(fEvV, 1)
                else:
                    di = d_index[m]
                    if di >= 2:
                        v.wait_ge(fTp, di - 1)
                    v.tensor_copy(stageE[:, di % 2, :], pf[m % 4][:, :]
                                  ).then_inc(fEvV, 1)
                    if di >= 1:
                        f_add(v, di - 1)
            f_add(v, NDG - 1)

    emit_G()
    es.close()
    return nc, dbg


_cache = {}


def _get_nc():
    key = (tuple(sorted(_DBG)), _STOP)
    if key not in _cache:
        _cache[key] = build_nc()
    return _cache[key]


def kernel(L0_attn_x, L0_attn_wq, L0_attn_bq, L0_attn_wk, L0_attn_bk,
           L0_attn_wv, L0_attn_bv, L0_attn_wo, L0_attn_bo,
           L0_ffn_Wr, L0_ffn_Wg, L0_ffn_Wu, L0_ffn_Wd, _extra_out=None):
    x = np.asarray(L0_attn_x, dtype=np.float32)
    shared = {
        "wqT": np.ascontiguousarray(np.asarray(L0_attn_wq, np.float32).T),
        "wkT": np.ascontiguousarray(np.asarray(L0_attn_wk, np.float32).T),
        "wvT": np.ascontiguousarray(np.asarray(L0_attn_wv, np.float32).T),
        "woT": np.ascontiguousarray(np.asarray(L0_attn_wo, np.float32).T),
        "b_q": np.ascontiguousarray(
            np.asarray(L0_attn_bq, np.float32).reshape(DT, P).T),
        "b_k": np.ascontiguousarray(
            np.asarray(L0_attn_bk, np.float32).reshape(DT, P).T),
        "b_v": np.ascontiguousarray(
            np.asarray(L0_attn_bv, np.float32).reshape(DT, P).T),
        "b_o": np.ascontiguousarray(
            np.asarray(L0_attn_bo, np.float32).reshape(DT, P).T),
        "wr": np.ascontiguousarray(np.asarray(L0_ffn_Wr, np.float32)),
        "wg": np.ascontiguousarray(np.asarray(L0_ffn_Wg, np.float32)),
        "wu": np.ascontiguousarray(np.asarray(L0_ffn_Wu, np.float32)),
        "wdn": np.ascontiguousarray(np.asarray(L0_ffn_Wd, np.float32)),
    }
    nc, dbg = _get_nc()
    in_maps = []
    for c in range(8):
        b, qb = divmod(c, 4)
        m = dict(shared)
        m["x_own"] = np.ascontiguousarray(x[b, qb * TOK:(qb + 1) * TOK])
        in_maps.append(m)
    res = run_bass_kernel_spmd(nc, in_maps, core_ids=list(range(8)))
    outf = np.empty((2, S, D), np.float32)
    for c in range(8):
        b, qb = divmod(c, 4)
        outf[b, qb * TOK:(qb + 1) * TOK] = res.results[c]["out"]
    if _extra_out is not None:
        for c in range(8):
            _extra_out.append({k: np.asarray(res.results[c][k]) for k in dbg})
    return outf


# version 12
# speedup vs baseline: 1.1758x; 1.0742x over previous
import math
from contextlib import ExitStack

import numpy as np

import concourse.bass as bass
import concourse.mybir as mybir
from concourse.bass_utils import run_bass_kernel_spmd

F32 = mybir.dt.float32
BF16 = mybir.dt.bfloat16
AF = mybir.ActivationFunctionType
ALU = mybir.AluOpType
AX = mybir.AxisListType

P = 128
D = 2048
TOK = 1024
S = 4096
NT = TOK // P
DT = D // P
H = 16
NKT = S // P
E = 8
INV_C = 1.0 / math.sqrt(128.0)
EPS = 1e-5

_DBG = set()
_STOP = "Z"


def _ap(t, offset, dims):
    return bass.AP(t, offset, dims)


def _ap_range(a):
    lo = a.offset
    ext = 1
    dims = list(a.ap)
    for step, count in dims[1:]:
        ext += abs(step) * (count - 1)
    return (a.tensor.name, lo, lo + ext)


def _overlaps(x, y):
    return x[0] == y[0] and x[1] < y[2] and y[1] < x[2]


class _SER:

    _WR = {"out", "accum_out"}
    _PASS = {"wait_ge", "drain", "nop", "sem_inc", "dma_start",
             "dma_start_transpose", "collective_compute", "wait_op",
             "memzero"}

    def __init__(self, e):
        self._e = e
        self._dw = []
        self._dr = []

    def __getattr__(self, name):
        f = getattr(self._e, name)
        if name in self._PASS or not callable(f):
            return f

        def wrapped(*args, **kw):
            ws, rs = [], []
            for i, a in enumerate(args):
                if isinstance(a, bass.AP):
                    (ws if i == 0 else rs).append(_ap_range(a))
            for k, a in kw.items():
                if isinstance(a, bass.AP):
                    (ws if k in self._WR else rs).append(_ap_range(a))
            conflict = any(_overlaps(t, w) for t in ws + rs
                           for w in self._dw) or                        any(_overlaps(w, r) for w in ws for r in self._dr)
            if conflict:
                self._e.drain()
                self._dw.clear()
                self._dr.clear()
            self._dw.extend(ws)
            self._dr.extend(rs)
            return f(*args, **kw)

        return wrapped


def build_nc():
    nc = bass.Bass("TRN2", target_bir_lowering=False)

    x_own = nc.dram_tensor("x_own", [TOK, D], F32, kind="ExternalInput")
    wqT = nc.dram_tensor("wqT", [D, D], F32, kind="ExternalInput")
    wkT = nc.dram_tensor("wkT", [D, D], F32, kind="ExternalInput")
    wvT = nc.dram_tensor("wvT", [D, D], F32, kind="ExternalInput")
    woT = nc.dram_tensor("woT", [D, D], F32, kind="ExternalInput")
    b_q = nc.dram_tensor("b_q", [P, DT], F32, kind="ExternalInput")
    b_k = nc.dram_tensor("b_k", [P, DT], F32, kind="ExternalInput")
    b_v = nc.dram_tensor("b_v", [P, DT], F32, kind="ExternalInput")
    b_o = nc.dram_tensor("b_o", [P, DT], F32, kind="ExternalInput")
    wr = nc.dram_tensor("wr", [D, E], F32, kind="ExternalInput")
    wg = nc.dram_tensor("wg", [E, D, D], F32, kind="ExternalInput")
    wu = nc.dram_tensor("wu", [E, D, D], F32, kind="ExternalInput")
    wdn = nc.dram_tensor("wdn", [E, D, D], F32, kind="ExternalInput")
    out = nc.dram_tensor("out", [TOK, D], F32, kind="ExternalOutput")

    q_loc = nc.dram_tensor("q_loc", [D, TOK], BF16, kind="Internal")
    k_loc = nc.dram_tensor("k_loc", [TOK, D], BF16, kind="Internal")
    v_loc = nc.dram_tensor("v_loc", [TOK, D], BF16, kind="Internal")
    k_full = nc.dram_tensor("k_full", [S, D], BF16, kind="Internal")
    v_full = nc.dram_tensor("v_full", [S, D], BF16, kind="Internal")

    dbg = {}
    for name, shape, dt in (
        ("dbg_q", [P, H, TOK], BF16),
        ("dbg_kv", [TOK, 2 * D], BF16),
        ("dbg_lnt", [P, DT, TOK], BF16),
        ("dbg_ctx", [P, H, TOK], BF16),
        ("dbg_x1", [P, NT, D], F32),
        ("dbg_w8", [P, E, NT], F32),
        ("dbg_ln2", [P, DT, TOK], BF16),
    ):
        if name in _DBG:
            dbg[name] = nc.dram_tensor(name, shape, dt, kind="ExternalOutput")

    es = ExitStack()
    X_res = es.enter_context(nc.sbuf_tensor("X_res", [P, NT, D], F32))
    actT = es.enter_context(nc.sbuf_tensor("actT", [P, DT, TOK], BF16))
    scr = es.enter_context(nc.sbuf_tensor("scr", [P, 16384], BF16))
    bq_sb = es.enter_context(nc.sbuf_tensor("bq_sb", [P, DT], F32))
    bk_sb = es.enter_context(nc.sbuf_tensor("bk_sb", [P, DT], F32))
    bv_sb = es.enter_context(nc.sbuf_tensor("bv_sb", [P, DT], F32))
    bo_sb = es.enter_context(nc.sbuf_tensor("bo_sb", [P, DT], F32))
    ident = es.enter_context(nc.sbuf_tensor("ident", [P, P], BF16))
    c_sum = es.enter_context(nc.sbuf_tensor("c_sum", [P, NT], F32))
    c_ssq = es.enter_context(nc.sbuf_tensor("c_ssq", [P, NT], F32))
    c_negm = es.enter_context(nc.sbuf_tensor("c_negm", [P, NT], F32))
    c_rstd = es.enter_context(nc.sbuf_tensor("c_rstd", [P, NT], F32))
    c_t0 = es.enter_context(nc.sbuf_tensor("c_t0", [P, NT], F32))
    c_t1 = es.enter_context(nc.sbuf_tensor("c_t1", [P, NT], F32))
    wr_sb = es.enter_context(nc.sbuf_tensor("wr_sb", [P, DT, E], F32))
    wr_hi = es.enter_context(nc.sbuf_tensor("wr_hi", [P, DT, E], BF16))
    wr_lo = es.enter_context(nc.sbuf_tensor("wr_lo", [P, DT, E], BF16))
    W_sel = es.enter_context(nc.sbuf_tensor("W_sel", [P, E, NT], F32))

    junk = scr[:, 0:D]
    xst = [scr[:, D + i * D: D + (i + 1) * D] for i in range(2)]
    kvst = [scr[:, 3 * D + i * 512: 3 * D + (i + 1) * 512] for i in range(4)]
    PT4 = scr[:, :].rearrange("p (a b) -> p a b", a=NKT)
    hlT = scr[:, :].rearrange("p (a b) -> p a b", a=DT)

    def sem(name):
        return es.enter_context(nc.semaphore(name))

    def emit_G():
        gO = sem("gO")
        with nc.Block() as blk:
            @blk.sync
            def _(sy):
                sy.dma_start(out=_ap(out, 0, [[D, P], [P * D, NT], [1, D]]),
                             in_=X_res[:, :, :]).then_inc(gO, 16)
                sy.wait_ge(gO, 16)

    def strip_src(mat, oc, expert=None):
        base = oc * 512
        if expert is not None:
            base += expert * D * D
        return _ap(mat, base, [[D, P], [P * D, DT], [1, 512]])

    aId = sem("aId")
    aX = sem("aX")
    aB = sem("aB")
    aWs = [sem("aW0"), sem("aW1")]
    aS1 = sem("aS1")
    aV2 = sem("aV2")
    aS2 = sem("aS2")
    aV3 = sem("aV3")
    aTt = sem("aTt")
    aEv = sem("aEv")
    aPm = sem("aPm")
    aPe = sem("aPe")
    aKVs = [sem(f"aKV{i}") for i in range(4)]
    ccK = sem("ccK")
    ccV = sem("ccV")


    a_strips = ([strip_src(wkT, oc) for oc in range(4)]
                + [strip_src(wvT, oc) for oc in range(4)]
                + [strip_src(wqT, oc) for oc in range(4)])
    NG = 96

    def a_group(idx):
        if idx < 32:
            oc, tt = divmod(idx, 8)
            return ("k", oc, oc, tt, None)
        if idx < 64:
            oc, tt = divmod(idx - 32, 8)
            return ("v", 4 + oc, oc, tt, None)
        qi = idx - 64
        oc, r = divmod(qi, 8)
        os_, g2 = divmod(r, 2)
        return ("q", 8 + oc, oc, os_, g2)

    with (
        nc.sbuf_tensor("wbufA0", [P, DT, 512], BF16) as wbA0,
        nc.sbuf_tensor("wbufA1", [P, DT, 512], BF16) as wbA1,
        nc.psum_tensor("tpA", [P, 8, P], BF16) as tpA,
        nc.psum_tensor("tpB", [P, 8, P], BF16) as tpB,
        nc.psum_tensor("tpC", [P, 8, P], BF16) as tpC,
        nc.psum_tensor("tpD", [P, 8, P], BF16) as tpD,
        nc.psum_tensor("pm0", [P, 512], F32) as pm0,
        nc.psum_tensor("pm1", [P, 512], F32) as pm1,
        nc.psum_tensor("pm2", [P, 512], F32) as pm2,
        nc.psum_tensor("pm3", [P, 512], F32) as pm3,
        nc.Block() as blk,
    ):
        wbuf = [wbA0, wbA1]
        tp = [[tpA, tpB], [tpC, tpD]]
        pmm = [pm0, pm1, pm2, pm3]

        @blk.sync
        def _(sy):
            sy.dma_start(out=X_res[:, :, :],
                         in_=_ap(x_own, 0, [[D, P], [P * D, NT], [1, D]])
                         ).then_inc(aX, 16)
            for bsrc, bdst in ((b_q, bq_sb), (b_k, bk_sb), (b_v, bv_sb),
                               (b_o, bo_sb)):
                sy.dma_start(out=bdst[:, :], in_=bsrc[:, :]).then_inc(aB, 16)
            sy.dma_start(out=wr_sb[:, :, :],
                         in_=_ap(wr, 0, [[E, P], [E * P, DT], [1, E]])
                         ).then_inc(aB, 16)
            for idx in range(96):
                kind, _, oc, a3, g2 = a_group(idx)
                sy.wait_ge(aPe, idx + 1)
                if kind in ("k", "v"):
                    dst = k_loc if kind == "k" else v_loc
                    out_ap = dst[a3 * P:(a3 + 1) * P,
                                 oc * 512:(oc + 1) * 512]
                else:
                    ot = oc * 4 + a3
                    out_ap = _ap(q_loc, ot * P * TOK + g2 * 512,
                                 [[TOK, P], [1, 512]])
                sy.dma_start(out=out_ap,
                             in_=kvst[idx % 4]).then_inc(aKVs[idx % 4], 16)
            sy.wait_ge(aB, 80)

        @blk.gpsimd
        def _(g):
            g = _SER(g)
            g.memset(ident[:, :], 0.0).then_inc(aId, 1)
            g.wait_ge(aId, 1)
            g.affine_select(out=ident[:, :], in_=ident[:, :],
                            compare_op=ALU.not_equal, fill=1.0, base=0,
                            pattern=[[-1, P]], channel_multiplier=1
                            ).then_inc(aId, 1)
            for i, src in enumerate(a_strips):
                if i == 8:
                    for q in range(4):
                        g.wait_ge(aKVs[q], 16 * 8)
                    g.collective_compute(
                        "AllGather", ALU.bypass,
                        replica_groups=[[0, 1, 2, 3], [4, 5, 6, 7]],
                        ins=[k_loc.ap().opt()], outs=[k_full.ap().opt()],
                    ).then_inc(ccK)
                if i >= 2:
                    g.wait_ge(aPe, (i - 2) * 8 + 8)
                g.dma_start(out=wbuf[i % 2][:, :, :], in_=src
                            ).then_inc(aWs[i % 2], 16)
            for q in range(4):
                g.wait_ge(aKVs[q], 16 * 16)
            g.collective_compute(
                "AllGather", ALU.bypass,
                replica_groups=[[0, 1, 2, 3], [4, 5, 6, 7]],
                ins=[v_loc.ap().opt()], outs=[v_full.ap().opt()],
            ).then_inc(ccV)

        @blk.scalar
        def _(s):
            s = _SER(s)
            s.wait_ge(aX, 16)
            for j in range(NT):
                s.activation(out=junk, in_=X_res[:, j, :], func=AF.Square,
                             accum_out=c_ssq[:, j:j + 1]).then_inc(aS1, 1)
            def a_evac(j):
                for half in range(2):
                    s.wait_ge(aTt, 2 * j + half + 1)
                    s.copy(
                        actT[:, half * 8:(half + 1) * 8, j * P:(j + 1) * P],
                        tp[j % 2][half][:, :, :]).then_inc(aEv, 1)
            for j in range(NT):
                s.wait_ge(aV2, j + 1)
                s.activation(out=c_rstd[:, j:j + 1], in_=c_t1[:, j:j + 1],
                             func=AF.Sqrt).then_inc(aS2, 1)
                if j >= 2:
                    a_evac(j - 2)
            a_evac(NT - 2)
            a_evac(NT - 1)

        @blk.vector
        def _(v):
            v = _SER(v)
            v.wait_ge(aX, 16)
            for j in range(NT):
                v.tensor_reduce(c_sum[:, j:j + 1], X_res[:, j, :], AX.X,
                                ALU.add)
                v.wait_ge(aS1, j + 1)
                v.tensor_scalar(c_t0[:, j:j + 1], c_sum[:, j:j + 1],
                                1.0 / D, None, ALU.mult)
                v.tensor_scalar(c_negm[:, j:j + 1], c_t0[:, j:j + 1],
                                -1.0, None, ALU.mult)
                v.tensor_scalar(c_t1[:, j:j + 1], c_ssq[:, j:j + 1],
                                1.0 / D, None, ALU.mult)
                v.scalar_tensor_tensor(c_t1[:, j:j + 1], c_t0[:, j:j + 1],
                                       c_negm[:, j:j + 1], c_t1[:, j:j + 1],
                                       ALU.mult, ALU.add)
                v.tensor_scalar(c_t1[:, j:j + 1], c_t1[:, j:j + 1],
                                EPS, None, ALU.add)
                v.reciprocal(c_t1[:, j:j + 1], c_t1[:, j:j + 1]
                             ).then_inc(aV2, 1)
                v.wait_ge(aS2, j + 1)
                if j >= 2:
                    v.wait_ge(aTt, 2 * (j - 2) + 2)
                v.tensor_scalar(xst[j % 2], X_res[:, j, :],
                                c_negm[:, j:j + 1], c_rstd[:, j:j + 1],
                                ALU.add, ALU.mult).then_inc(aV3, 1)
            for idx in range(NG):
                kind, _, oc, a3, g2 = a_group(idx)
                v.wait_ge(aPm, idx + 1)
                b = idx % 4
                if kind in ("k", "v"):
                    if idx >= 4:
                        v.wait_ge(aKVs[idx % 4], 16 * (idx // 4))
                    v.tensor_copy(kvst[idx % 4], pmm[b][:, :]).then_inc(aPe, 1)
                else:
                    ot = oc * 4 + a3
                    if idx == 64:
                        v.wait_ge(aB, 80)
                    if idx >= 4:
                        v.wait_ge(aKVs[idx % 4], 16 * (idx // 4))
                    v.tensor_scalar(kvst[idx % 4],
                                    pmm[b][:, :], bq_sb[:, ot:ot + 1],
                                    INV_C, ALU.add, ALU.mult).then_inc(aPe, 1)

        @blk.tensor
        def _(t):
            t.wait_ge(aId, 2)
            for j in range(NT):
                t.wait_ge(aV3, j + 1)
                if j >= 2:
                    t.wait_ge(aEv, 2 * (j - 2) + 2)
                for dt in range(DT):
                    bank = tp[j % 2][dt // 8]
                    ins = t.transpose(bank[:, dt % 8, :],
                                      xst[j % 2][:, dt * P:(dt + 1) * P],
                                      ident[:, :])
                    if dt % 8 == 7:
                        ins.then_inc(aTt, 1)
            for idx in range(NG):
                kind, strip, oc, a3, g2 = a_group(idx)
                t.wait_ge(aWs[strip % 2], 16 * (strip // 2 + 1))
                if kind in ("k", "v"):
                    t.wait_ge(aEv, 2 * a3 + 2)
                else:
                    t.wait_ge(aEv, 2 * NT)
                if idx >= 4:
                    t.wait_ge(aPe, idx - 3)
                b = idx % 4
                for dt in range(DT):
                    if kind in ("k", "v"):
                        ins = t.matmul(pmm[b][:, :],
                                       actT[:, dt, a3 * P:(a3 + 1) * P],
                                       wbuf[strip % 2][:, dt, :],
                                       start=(dt == 0), stop=(dt == DT - 1))
                    else:
                        ins = t.matmul(
                            pmm[b][:, :],
                            wbuf[strip % 2][:, dt, a3 * P:(a3 + 1) * P],
                            actT[:, dt, g2 * 512:(g2 + 1) * 512],
                            start=(dt == 0), stop=(dt == DT - 1))
                ins.then_inc(aPm, 1)

    if {"dbg_kv", "dbg_q", "dbg_lnt"} & set(dbg):
        dS = sem("dS")
        with nc.Block() as blk:
            @blk.sync
            def _(sy):
                n = 0
                for key, src in (("dbg_kv", None),
                                 ("dbg_lnt", actT[:, :, :])):
                    if key == "dbg_kv" and key in dbg:
                        sy.dma_start(out=dbg[key].ap()[:, 0:D], in_=k_loc.ap()
                                     ).then_inc(dS, 16)
                        sy.dma_start(out=dbg[key].ap()[:, D:2 * D],
                                     in_=v_loc.ap()).then_inc(dS, 16)
                        n += 32
                        continue
                    if key in dbg:
                        sy.dma_start(out=dbg[key].ap(), in_=src
                                     ).then_inc(dS, 16)
                        n += 16
                sy.wait_ge(dS, n)

    if _STOP <= "B":
        emit_G()
        es.close()
        return nc, dbg
    cKHs = [sem("cKH0"), sem("cKH1")]
    cKb = sem("cKb")
    cVHs = [sem("cVH0"), sem("cVH1")]
    cSm = sem("cSm")
    cMx = sem("cMx")
    cMl = sem("cMl")
    cEx = sem("cEx")
    cFa = sem("cFa")
    cNm = sem("cNm")
    cPTs = [sem(f"cPT{i}") for i in range(4)]
    cPV = sem("cPV")
    cCx = sem("cCx")

    cQHs = [sem("cQH0"), sem("cQH1")]
    with (
        nc.sbuf_tensor("Q_h", [P, 2, TOK], BF16) as Q_h,
        nc.sbuf_tensor("K_h", [P, 2, S], BF16) as K_h,
        nc.sbuf_tensor("V_h", [P, 2, NKT, P], BF16) as V_h,
        nc.sbuf_tensor("P_buf", [P, 4, S], BF16) as P_buf,
        nc.sbuf_tensor("negm8", [P, 2, 8], F32) as negm8,
        nc.sbuf_tensor("sum8", [P, 2, 8], F32) as sum8,
        nc.sbuf_tensor("fac8", [P, 2, 8], F32) as fac8,
        nc.sbuf_tensor("c_mall", [P, 2], F32) as c_mall,
        nc.sbuf_tensor("c_sall", [P, 2], F32) as c_sall,
        nc.sbuf_tensor("junk8", [P, 8], F32) as junk8,
        nc.psum_tensor("ps_s", [P, 6, 512], F32) as ps_s,
        nc.psum_tensor("ps_c0", [P, 512], F32) as ps_c0,
        nc.psum_tensor("ps_c1", [P, 512], F32) as ps_c1,
        nc.Block() as blk,
    ):
        ps_c = [ps_c0, ps_c1]

        @blk.sync
        def _(sy):
            def k_load(h):
                if h >= H:
                    return
                if h == 0:
                    sy.wait_ge(ccK, 1)
                if h >= 2:
                    sy.wait_ge(cSm, 64 * (h - 1))
                sy.dma_start_transpose(
                    out=K_h[:, h % 2, :],
                    in_=k_full[:, h * P:(h + 1) * P]).then_inc(cKHs[h % 2], 16)

            for q in range(4):
                sy.wait_ge(aKVs[q], 16 * 24)
            k_load(0)
            k_load(1)
            for gq in range(H * NT):
                sl, gb = gq % 4, gq // 4
                sy.wait_ge(cNm, gq * 4 + 4)
                sy.wait_ge(cPV, gb)
                sy.dma_start_transpose(
                    out=PT4[:, :, sl * P:(sl + 1) * P],
                    in_=P_buf[:, gq % 4, :].rearrange(
                        "q (kt p) -> q kt p", kt=NKT)
                    ).then_inc(cPTs[sl], 16)
                if gq % 8 == 5:
                    k_load(gq // 8 + 2)

        @blk.scalar
        def _(s):
            s = _SER(s)
            for q in range(4):
                s.wait_ge(aKVs[q], 16 * 24)
            for h in range(H):
                if h >= 2:
                    s.wait_ge(cSm, 64 * (h - 1))
                s.dma_start(out=Q_h[:, h % 2, :],
                            in_=q_loc[h * P:(h + 1) * P, :]
                            ).then_inc(cQHs[h % 2], 16)
                if h == 0:
                    s.wait_ge(ccV, 1)
                if h >= 2:
                    s.wait_ge(cPV, 2 * h - 2)
                s.dma_start(
                    out=V_h[:, h % 2, :, :],
                    in_=_ap(v_full, h * P, [[D, P], [P * D, NKT], [1, P]])
                    ).then_inc(cVHs[h % 2], 16)
                for jq in range(NT):
                    gq = h * NT + jq
                    jr = jq % 2
                    if gq >= 4:
                        psl = (gq - 4) % 4
                        s.wait_ge(cPTs[psl], 16 * ((gq - 4) // 4 + 1))
                    if gq >= 2:
                        s.wait_ge(cFa, gq - 1)
                    for kp in range(4):
                        pl = gq * 4 + kp
                        b0 = 2 * (pl % 3)
                        s.wait_ge(cMx, pl + 1)
                        s.activation(out=P_buf[:, gq % 4,
                                                kp * 1024:(kp + 1) * 1024],
                                     in_=ps_s[:, b0:b0 + 2, :], func=AF.Exp,
                                     bias=negm8[:, jr, kp:kp + 1], scale=1.0,
                                     accum_out=sum8[:, jr, kp:kp + 1]
                                     ).then_inc(cEx, 1)
                    s.wait_ge(cMl, gq + 1)
                    s.activation(out=fac8[:, jr, 0:4], in_=negm8[:, jr, 0:4],
                                 func=AF.Exp, bias=c_mall[:, jr:jr + 1],
                                 scale=-1.0).then_inc(cEx, 1)

        @blk.vector
        def _(v):
            v = _SER(v)

            def v_ctx_evac(kb):
                v.wait_ge(cPV, kb + 1)
                v.tensor_scalar(actT[:, kb // 2,
                                     (kb % 2) * 512:(kb % 2) * 512 + 512],
                                ps_c[kb % 2][:, :],
                                bv_sb[:, kb // 2:kb // 2 + 1],
                                None, ALU.add).then_inc(cCx, 1)

            def v_stats_norm(gq):
                jr = (gq % NT) % 2
                v.wait_ge(cEx, gq * 5 + 5)
                v.tensor_mul(junk8[:, 0:4], sum8[:, jr, 0:4],
                             fac8[:, jr, 0:4])
                v.tensor_reduce(c_sall[:, jr:jr + 1], junk8[:, 0:4],
                                AX.X, ALU.add)
                v.reciprocal(c_sall[:, jr:jr + 1], c_sall[:, jr:jr + 1])
                v.tensor_scalar(fac8[:, jr, 0:4], fac8[:, jr, 0:4],
                                c_sall[:, jr:jr + 1], None, ALU.mult
                                ).then_inc(cFa, 1)
                for kp in range(4):
                    v.tensor_scalar(P_buf[:, gq % 4,
                                          kp * 1024:(kp + 1) * 1024],
                                    P_buf[:, gq % 4,
                                          kp * 1024:(kp + 1) * 1024],
                                    fac8[:, jr, kp:kp + 1], None,
                                    ALU.mult).then_inc(cNm, 1)

            prev = None
            for h in range(H):
                v.wait_ge(cKHs[h % 2], 16 * (h // 2 + 1))
                if h == 0:
                    v.wait_ge(aB, 80)
                v.tensor_scalar(K_h[:, h % 2, :], K_h[:, h % 2, :],
                                bk_sb[:, h:h + 1], None, ALU.add
                                ).then_inc(cKb, 1)
                for jq in range(NT):
                    gq = h * NT + jq
                    jr = jq % 2
                    for kp in range(4):
                        v.wait_ge(cSm, gq * 8 + 2 * kp + 2)
                        pl = gq * 4 + kp
                        b0 = 2 * (pl % 3)
                        v.tensor_reduce(negm8[:, jr, kp:kp + 1],
                                        ps_s[:, b0:b0 + 2, :],
                                        AX.XY, ALU.max,
                                        negate=True).then_inc(cMx, 1)
                    v.tensor_reduce(c_mall[:, jr:jr + 1], negm8[:, jr, 0:4],
                                    AX.X, ALU.min).then_inc(cMl, 1)
                    if prev is not None:
                        v_stats_norm(prev)
                    prev = gq
                    if gq >= 5 and (gq - 5) % 4 == 0:
                        v_ctx_evac((gq - 5) // 4)
            v_stats_norm(prev)
            v_ctx_evac(30)
            v_ctx_evac(31)

        @blk.tensor
        def _(t):
            def t_pv(gb):
                hh = gb // 2
                for q in range(4):
                    t.wait_ge(cPTs[q], 16 * (gb + 1))
                if gb >= 2:
                    t.wait_ge(cCx, gb - 1)
                for kt in range(NKT):
                    ins = t.matmul(ps_c[gb % 2][:, :],
                                   V_h[:, hh % 2, kt, :],
                                   PT4[:, kt, :],
                                   start=(kt == 0),
                                   stop=(kt == NKT - 1))
                ins.then_inc(cPV, 1)

            for h in range(H):
                t.wait_ge(cKb, h + 1)
                t.wait_ge(cQHs[h % 2], 16 * (h // 2 + 1))
                t.wait_ge(cVHs[h % 2], 16 * (h // 2 + 1))
                for jq in range(NT):
                    gq = h * NT + jq
                    for kc in range(8):
                        pl = gq * 4 + kc // 2
                        if kc % 2 == 0 and pl >= 3:
                            pg, pp = divmod(pl - 3, 4)
                            t.wait_ge(cEx, pg * 5 + pp + 1)
                        bank = (2 * (pl % 3)) + (kc % 2)
                        t.matmul(ps_s[:, bank, :],
                                 Q_h[:, h % 2, jq * P:(jq + 1) * P],
                                 K_h[:, h % 2, kc * 512:(kc + 1) * 512],
                                 start=True, stop=True).then_inc(cSm, 1)
                    if gq % 4 == 0 and gq >= 4:
                        t_pv(gq // 4 - 1)
            t_pv(31)


    if "dbg_ctx" in dbg:
        dC = sem("dC")
        with nc.Block() as blk:
            @blk.sync
            def _(sy):
                sy.dma_start(out=dbg["dbg_ctx"].ap(), in_=actT[:, :, :]
                             ).then_inc(dC, 16)
                sy.wait_ge(dC, 16)

    if _STOP <= "C":
        emit_G()
        es.close()
        return nc, dbg

    dWs = [sem("dW0"), sem("dW1")]
    dMM = sem("dMM")
    dEv = sem("dEv")
    dTp = sem("dTp")
    dAdd = sem("dAdd")

    with (
        nc.sbuf_tensor("wbufD0", [P, DT, 512], BF16) as wbD0,
        nc.sbuf_tensor("wbufD1", [P, DT, 512], BF16) as wbD1,
        nc.sbuf_tensor("stageO", [P, 2, 512], BF16) as stageO,
        nc.psum_tensor("pd0", [P, 512], F32) as pd0,
        nc.psum_tensor("pd1", [P, 512], F32) as pd1,
        nc.psum_tensor("ptd0", [P, 4, P], BF16) as ptd0,
        nc.psum_tensor("ptd1", [P, 4, P], BF16) as ptd1,
        nc.Block() as blk,
    ):
        wbufD = [wbD0, wbD1]
        pd = [pd0, pd1]
        ptd = [ptd0, ptd1]

        def d_tp(t, pidx):
            t.wait_ge(dEv, pidx + 1)
            if pidx >= 2:
                t.wait_ge(dAdd, pidx - 1)
            for k in range(4):
                ins = t.transpose(ptd[pidx % 2][:, k, :],
                                  stageO[:, pidx % 2, k * P:(k + 1) * P],
                                  ident[:, :])
            ins.then_inc(dTp, 1)

        def d_add(v, pidx):
            poc, pr = divmod(pidx, 8)
            pos, pg2 = divmod(pr, 2)
            pot = poc * 4 + pos
            v.wait_ge(dTp, pidx + 1)
            v.tensor_tensor(
                X_res[:, pg2 * 4:pg2 * 4 + 4, pot * P:(pot + 1) * P],
                ptd[pidx % 2][:, :, :],
                X_res[:, pg2 * 4:pg2 * 4 + 4, pot * P:(pot + 1) * P],
                ALU.add).then_inc(dAdd, 1)

        @blk.gpsimd
        def _(g):
            g = _SER(g)
            for oc in range(4):
                if oc >= 2:
                    g.wait_ge(dEv, (oc - 2) * 8 + 8)
                g.dma_start(out=wbufD[oc % 2][:, :, :],
                            in_=strip_src(woT, oc)).then_inc(dWs[oc % 2], 16)

        @blk.tensor
        def _(t):
            for idx in range(32):
                oc, r = divmod(idx, 8)
                os_, g2 = divmod(r, 2)
                t.wait_ge(dWs[oc % 2], 16 * (oc // 2 + 1))
                if idx >= 2:
                    t.wait_ge(dEv, idx - 1)
                for ct in range(DT):
                    ins = t.matmul(pd[idx % 2][:, :],
                                   wbufD[oc % 2][:, ct, os_ * P:(os_ + 1) * P],
                                   actT[:, ct, g2 * 512:(g2 + 1) * 512],
                                   start=(ct == 0), stop=(ct == DT - 1))
                ins.then_inc(dMM, 1)
                if idx >= 1:
                    d_tp(t, idx - 1)
            d_tp(t, 31)

        @blk.vector
        def _(v):
            v = _SER(v)
            for idx in range(32):
                oc, r = divmod(idx, 8)
                os_, g2 = divmod(r, 2)
                ot = oc * 4 + os_
                v.wait_ge(dMM, idx + 1)
                if idx >= 2:
                    v.wait_ge(dTp, idx - 1)
                v.tensor_scalar(stageO[:, idx % 2, :], pd[idx % 2][:, :],
                                bo_sb[:, ot:ot + 1], None, ALU.add
                                ).then_inc(dEv, 1)
                if idx >= 1:
                    d_add(v, idx - 1)
            d_add(v, 31)

    if "dbg_x1" in dbg:
        dX = sem("dX")
        with nc.Block() as blk:
            @blk.sync
            def _(sy):
                sy.dma_start(out=dbg["dbg_x1"].ap(), in_=X_res[:, :, :]
                             ).then_inc(dX, 16)
                sy.wait_ge(dX, 16)

    if _STOP <= "D":
        emit_G()
        es.close()
        return nc, dbg

    eS1 = sem("eS1")
    eV2 = sem("eV2")
    eS2 = sem("eS2")
    eV3 = sem("eV3")
    eTt = sem("eTt")
    eEv = sem("eEv")
    eRm = sem("eRm")
    eRe = sem("eRe")
    eRx = sem("eRx")
    eRw = sem("eRw")

    with (
        nc.sbuf_tensor("stgf", [P, 2, D], F32) as stgf,
        nc.sbuf_tensor("stghi", [P, 2, D], BF16) as stghi,
        nc.sbuf_tensor("stglo", [P, 2, D], BF16) as stglo,
        nc.sbuf_tensor("junkE", [P, D], BF16) as junkE,
        nc.sbuf_tensor("lg_sb", [P, NT, E], F32) as lg_sb,
        nc.sbuf_tensor("e8", [P, E], F32) as e8,
        nc.sbuf_tensor("r_m1", [P, 1], F32) as r_m1,
        nc.sbuf_tensor("r_nm1", [P, 1], F32) as r_nm1,
        nc.sbuf_tensor("r_m2", [P, 1], F32) as r_m2,
        nc.sbuf_tensor("r_msk", [P, E], F32) as r_msk,
        nc.sbuf_tensor("r_sel", [P, E], F32) as r_sel,
        nc.sbuf_tensor("r_wun", [P, E], F32) as r_wun,
        nc.sbuf_tensor("r_ss", [P, 1], F32) as r_ss,
        nc.psum_tensor("tpE0", [P, 8, P], BF16) as tpE0,
        nc.psum_tensor("tpE1", [P, 8, P], BF16) as tpE1,
        nc.psum_tensor("tpE2", [P, 8, P], BF16) as tpE2,
        nc.psum_tensor("tpE3", [P, 8, P], BF16) as tpE3,
        nc.psum_tensor("prt", [P, E], F32) as prt,
        nc.Block() as blk,
    ):
        tpE = [tpE0, tpE1, tpE2, tpE3]

        @blk.scalar
        def _(s):
            s = _SER(s)
            for j in range(NT):
                s.activation(out=junkE[:, :], in_=X_res[:, j, :], func=AF.Square,
                             accum_out=c_ssq[:, j:j + 1]).then_inc(eS1, 1)
            def e_evac(j):
                for q in range(4):
                    s.wait_ge(eTt, 4 * j + q + 1)
                    dst = actT if q < 2 else hlT
                    half = q % 2
                    s.copy(
                        dst[:, half * 8:(half + 1) * 8, j * P:(j + 1) * P],
                        tpE[q][:, :, :]).then_inc(eEv, 1)
            for j in range(NT):
                s.wait_ge(eV2, j + 1)
                s.activation(out=c_rstd[:, j:j + 1], in_=c_t1[:, j:j + 1],
                             func=AF.Sqrt).then_inc(eS2, 1)
                if j >= 2:
                    e_evac(j - 2)
            e_evac(NT - 2)
            e_evac(NT - 1)
            for j in range(NT):
                s.wait_ge(eRe, 3 * j + 3)
                s.activation(out=e8[:, :], in_=lg_sb[:, j, :], func=AF.Exp,
                             bias=r_nm1[:, :], scale=1.0).then_inc(eRx, 1)

        @blk.vector
        def _(v):
            v = _SER(v)
            for j in range(NT):
                v.tensor_reduce(c_sum[:, j:j + 1], X_res[:, j, :], AX.X,
                                ALU.add)
                v.wait_ge(eS1, j + 1)
                v.tensor_scalar(c_t0[:, j:j + 1], c_sum[:, j:j + 1],
                                1.0 / D, None, ALU.mult)
                v.tensor_scalar(c_negm[:, j:j + 1], c_t0[:, j:j + 1],
                                -1.0, None, ALU.mult)
                v.tensor_scalar(c_t1[:, j:j + 1], c_ssq[:, j:j + 1],
                                1.0 / D, None, ALU.mult)
                v.scalar_tensor_tensor(c_t1[:, j:j + 1], c_t0[:, j:j + 1],
                                       c_negm[:, j:j + 1], c_t1[:, j:j + 1],
                                       ALU.mult, ALU.add)
                v.tensor_scalar(c_t1[:, j:j + 1], c_t1[:, j:j + 1],
                                EPS, None, ALU.add)
                v.reciprocal(c_t1[:, j:j + 1], c_t1[:, j:j + 1]
                             ).then_inc(eV2, 1)
                v.wait_ge(eS2, j + 1)
                if j >= 2:
                    v.wait_ge(eTt, 4 * (j - 2) + 4)
                v.tensor_scalar(stgf[:, j % 2, :], X_res[:, j, :],
                                c_negm[:, j:j + 1], c_rstd[:, j:j + 1],
                                ALU.add, ALU.mult)
                v.tensor_copy(stghi[:, j % 2, :], stgf[:, j % 2, :]
                              ).then_inc(eV3, 1)
                v.tensor_sub(stglo[:, j % 2, :], stgf[:, j % 2, :],
                             stghi[:, j % 2, :]).then_inc(eV3, 1)
            v.tensor_copy(wr_hi[:, :, :], wr_sb[:, :, :])
            v.tensor_sub(wr_lo[:, :, :], wr_sb[:, :, :], wr_hi[:, :, :]
                         ).then_inc(eV3, 1)
            for j in range(NT):
                v.wait_ge(eRm, j + 1)
                v.tensor_copy(lg_sb[:, j, :], prt[:, :]).then_inc(eRe, 1)
                v.tensor_reduce(r_m1[:, :], lg_sb[:, j, :], AX.X, ALU.max)
                if j >= 1:
                    v.wait_ge(eRx, j)
                v.tensor_scalar(r_nm1[:, :], r_m1[:, :], -1.0, None,
                                ALU.mult).then_inc(eRe, 1)
                v.tensor_scalar(r_msk[:, :], lg_sb[:, j, :], r_m1[:, :],
                                -1e30, ALU.is_ge, ALU.mult)
                v.tensor_add(r_msk[:, :], lg_sb[:, j, :], r_msk[:, :])
                v.tensor_reduce(r_m2[:, :], r_msk[:, :], AX.X, ALU.max
                                ).then_inc(eRe, 1)
                v.wait_ge(eRx, j + 1)
                v.tensor_scalar(r_sel[:, :], lg_sb[:, j, :], r_m2[:, :],
                                None, ALU.is_ge)
                v.tensor_mul(r_wun[:, :], e8[:, :], r_sel[:, :])
                v.tensor_reduce(r_ss[:, :], r_wun[:, :], AX.X, ALU.add)
                v.reciprocal(r_ss[:, :], r_ss[:, :])
                v.tensor_scalar(W_sel[:, :, j], r_wun[:, :], r_ss[:, :],
                                None, ALU.mult).then_inc(eRw, 1)

        @blk.tensor
        def _(t):
            for j in range(NT):
                for pas in range(2):
                    t.wait_ge(eV3, 2 * j + pas + 1)
                    if j >= 1:
                        t.wait_ge(eEv, 4 * (j - 1) + 2 * pas + 2)
                    src = stghi if pas == 0 else stglo
                    for dt in range(DT):
                        bank = tpE[2 * pas + dt // 8]
                        ins = t.transpose(bank[:, dt % 8, :],
                                          src[:, j % 2, dt * P:(dt + 1) * P],
                                          ident[:, :])
                        if dt % 8 == 7:
                            ins.then_inc(eTt, 1)
            t.wait_ge(eV3, 2 * NT + 1)
            t.wait_ge(eEv, 4 * NT)
            for j in range(NT):
                if j >= 1:
                    t.wait_ge(eRe, 3 * (j - 1) + 1)
                nmm = 0
                for lhs_src, rhs_src in ((actT, wr_hi), (actT, wr_lo),
                                         (hlT, wr_hi)):
                    for dt in range(DT):
                        ins = t.matmul(prt[:, :],
                                       lhs_src[:, dt, j * P:(j + 1) * P],
                                       rhs_src[:, dt, :],
                                       start=(nmm == 0), stop=(nmm == 47))
                        nmm += 1
                ins.then_inc(eRm, 1)

    if {"dbg_w8", "dbg_ln2"} & set(dbg):
        dW8 = sem("dW8")
        with nc.Block() as blk:
            @blk.sync
            def _(sy):
                n = 0
                for key, src in (("dbg_w8", W_sel[:, :, :]),
                                 ("dbg_ln2", actT[:, :, :])):
                    if key in dbg:
                        sy.dma_start(out=dbg[key].ap(), in_=src
                                     ).then_inc(dW8, 16)
                        n += 16
                sy.wait_ge(dW8, n)

    if _STOP <= "E":
        emit_G()
        es.close()
        return nc, dbg

    fWs = [sem("fW0"), sem("fW1")]
    fMM = sem("fMM")
    fEvG = sem("fEvG")
    fEvV = sem("fEvV")
    fTp = sem("fTp")
    fAdd = sem("fAdd")

    f_strips = []
    for i in range(E):
        for fc in range(4):
            f_strips.append(strip_src(wg, fc, i))
            f_strips.append(strip_src(wu, fc, i))
        for oc in range(4):
            f_strips.append(strip_src(wdn, oc, i))

    fgroups = []
    for i in range(E):
        for fc in range(4):
            for fs in range(4):
                for g2 in range(2):
                    fgroups.append(("g", i, fc, fs, g2, i * 12 + 2 * fc))
            for fs in range(4):
                for g2 in range(2):
                    fgroups.append(("u", i, fc, fs, g2, i * 12 + 2 * fc + 1))
        for oc in range(4):
            for os_ in range(4):
                for g2 in range(2):
                    fgroups.append(("d", i, oc, os_, g2, i * 12 + 8 + oc))
    NFG = len(fgroups)
    evac_of = []
    nG = nV = 0
    d_index = {}
    g_evac_idx = {}
    u_evac_idx = {}
    ndi = 0
    for m, (kind, i, c, s_, g2, si) in enumerate(fgroups):
        if kind == "g":
            nG += 1
            evac_of.append(("G", nG))
            g_evac_idx[(i, c, s_, g2)] = nG
        else:
            nV += 1
            evac_of.append(("V", nV))
            if kind == "u":
                u_evac_idx[(i, c, s_, g2)] = nV
            else:
                d_index[m] = ndi
                ndi += 1
    NDG = ndi
    d_list = [m for m in range(NFG) if fgroups[m][0] == "d"]
    strip_last = {}
    for m, grp in enumerate(fgroups):
        strip_last[grp[5]] = m

    with (
        nc.sbuf_tensor("wbufF0", [P, DT, 512], BF16) as wbF0,
        nc.sbuf_tensor("wbufF1", [P, DT, 512], BF16) as wbF1,
        nc.sbuf_tensor("sg_buf", [P, 4, TOK], BF16) as sg_buf,
        nc.sbuf_tensor("stageE", [P, 2, 512], BF16) as stageE,
        nc.psum_tensor("pf0", [P, 512], F32) as pf0,
        nc.psum_tensor("pf1", [P, 512], F32) as pf1,
        nc.psum_tensor("pf2", [P, 512], F32) as pf2,
        nc.psum_tensor("pf3", [P, 512], F32) as pf3,
        nc.psum_tensor("ptf0", [P, 4, P], BF16) as ptf0,
        nc.psum_tensor("ptf1", [P, 4, P], BF16) as ptf1,
        nc.Block() as blk,
    ):
        wbufF = [wbF0, wbF1]
        pf = [pf0, pf1, pf2, pf3]
        ptf = [ptf0, ptf1]

        def f_tp(t, di):
            m = d_list[di]
            _, ei = evac_of[m]
            t.wait_ge(fEvV, ei)
            if di >= 2:
                t.wait_ge(fAdd, 4 * (di - 2) + 4)
            for k in range(4):
                ins = t.transpose(ptf[di % 2][:, k, :],
                                  stageE[:, di % 2, k * P:(k + 1) * P],
                                  ident[:, :])
            ins.then_inc(fTp, 1)

        def f_add(v, di):
            m = d_list[di]
            _, i, oc, os_, g2, _ = fgroups[m]
            ot = oc * 4 + os_
            v.wait_ge(fTp, di + 1)
            for k in range(4):
                j = g2 * 4 + k
                v.scalar_tensor_tensor(
                    X_res[:, j, ot * P:(ot + 1) * P],
                    ptf[di % 2][:, k, :], W_sel[:, i, j:j + 1],
                    X_res[:, j, ot * P:(ot + 1) * P],
                    ALU.mult, ALU.add).then_inc(fAdd, 1)

        @blk.gpsimd
        def _(g):
            g = _SER(g)
            for si, src in enumerate(f_strips):
                if si >= 2:
                    ek, ei = evac_of[strip_last[si - 2]]
                    g.wait_ge(fEvG if ek == "G" else fEvV, ei)
                g.dma_start(out=wbufF[si % 2][:, :, :], in_=src
                            ).then_inc(fWs[si % 2], 16)

        @blk.tensor
        def _(t):
            for m, (kind, i, c, s_, g2, si) in enumerate(fgroups):
                t.wait_ge(fWs[si % 2], 16 * (si // 2 + 1))
                if m >= 4:
                    ek, ei = evac_of[m - 4]
                    t.wait_ge(fEvG if ek == "G" else fEvV, ei)
                if kind == "d" and c == 0 and s_ == 0 and g2 == 0:
                    t.wait_ge(fEvV, u_evac_idx[(i, 3, 3, 1)])
                b = m % 4
                rhs_base = actT if kind in ("g", "u") else hlT
                for dt in range(DT):
                    ins = t.matmul(pf[b][:, :],
                                   wbufF[si % 2][:, dt, s_ * P:(s_ + 1) * P],
                                   rhs_base[:, dt, g2 * 512:(g2 + 1) * 512],
                                   start=(dt == 0), stop=(dt == DT - 1))
                ins.then_inc(fMM, 1)
                if kind == "d":
                    di = d_index[m]
                    if di >= 1:
                        f_tp(t, di - 1)
            f_tp(t, NDG - 1)

        @blk.scalar
        def _(s):
            s = _SER(s)
            for m, (kind, i, c, s_, g2, si) in enumerate(fgroups):
                if kind != "g":
                    continue
                s.wait_ge(fMM, m + 1)
                if c >= 1:
                    s.wait_ge(fEvV, u_evac_idx[(i, c - 1, s_, g2)])
                elif i >= 1:
                    s.wait_ge(fEvV, u_evac_idx[(i - 1, 3, s_, g2)])
                s.activation(out=sg_buf[:, s_, g2 * 512:(g2 + 1) * 512],
                             in_=pf[m % 4][:, :], func=AF.Silu
                             ).then_inc(fEvG, 1)

        @blk.vector
        def _(v):
            v = _SER(v)
            for m, (kind, i, c, s_, g2, si) in enumerate(fgroups):
                if kind == "g":
                    continue
                v.wait_ge(fMM, m + 1)
                if kind == "u":
                    v.wait_ge(fEvG, g_evac_idx[(i, c, s_, g2)])
                    v.tensor_tensor(
                        hlT[:, c * 4 + s_, g2 * 512:(g2 + 1) * 512],
                        sg_buf[:, s_, g2 * 512:(g2 + 1) * 512],
                        pf[m % 4][:, :], ALU.mult).then_inc(fEvV, 1)
                else:
                    di = d_index[m]
                    if di >= 2:
                        v.wait_ge(fTp, di - 1)
                    v.tensor_copy(stageE[:, di % 2, :], pf[m % 4][:, :]
                                  ).then_inc(fEvV, 1)
                    if di >= 1:
                        f_add(v, di - 1)
            f_add(v, NDG - 1)

    emit_G()
    es.close()
    return nc, dbg


_cache = {}


def _get_nc():
    key = (tuple(sorted(_DBG)), _STOP)
    if key not in _cache:
        _cache[key] = build_nc()
    return _cache[key]


def kernel(L0_attn_x, L0_attn_wq, L0_attn_bq, L0_attn_wk, L0_attn_bk,
           L0_attn_wv, L0_attn_bv, L0_attn_wo, L0_attn_bo,
           L0_ffn_Wr, L0_ffn_Wg, L0_ffn_Wu, L0_ffn_Wd, _extra_out=None):
    x = np.asarray(L0_attn_x, dtype=np.float32)
    shared = {
        "wqT": np.ascontiguousarray(np.asarray(L0_attn_wq, np.float32).T),
        "wkT": np.ascontiguousarray(np.asarray(L0_attn_wk, np.float32).T),
        "wvT": np.ascontiguousarray(np.asarray(L0_attn_wv, np.float32).T),
        "woT": np.ascontiguousarray(np.asarray(L0_attn_wo, np.float32).T),
        "b_q": np.ascontiguousarray(
            np.asarray(L0_attn_bq, np.float32).reshape(DT, P).T),
        "b_k": np.ascontiguousarray(
            np.asarray(L0_attn_bk, np.float32).reshape(DT, P).T),
        "b_v": np.ascontiguousarray(
            np.asarray(L0_attn_bv, np.float32).reshape(DT, P).T),
        "b_o": np.ascontiguousarray(
            np.asarray(L0_attn_bo, np.float32).reshape(DT, P).T),
        "wr": np.ascontiguousarray(np.asarray(L0_ffn_Wr, np.float32)),
        "wg": np.ascontiguousarray(np.asarray(L0_ffn_Wg, np.float32)),
        "wu": np.ascontiguousarray(np.asarray(L0_ffn_Wu, np.float32)),
        "wdn": np.ascontiguousarray(np.asarray(L0_ffn_Wd, np.float32)),
    }
    nc, dbg = _get_nc()
    in_maps = []
    for c in range(8):
        b, qb = divmod(c, 4)
        m = dict(shared)
        m["x_own"] = np.ascontiguousarray(x[b, qb * TOK:(qb + 1) * TOK])
        in_maps.append(m)
    res = run_bass_kernel_spmd(nc, in_maps, core_ids=list(range(8)))
    outf = np.empty((2, S, D), np.float32)
    for c in range(8):
        b, qb = divmod(c, 4)
        outf[b, qb * TOK:(qb + 1) * TOK] = res.results[c]["out"]
    if _extra_out is not None:
        for c in range(8):
            _extra_out.append({k: np.asarray(res.results[c][k]) for k in dbg})
    return outf
